# version 05
# baseline (speedup 1.0000x reference)
import numpy as np
import ml_dtypes
import concourse.bass as bass
import concourse.mybir as mybir
from concourse.bass_utils import run_bass_kernel_spmd
from contextlib import ExitStack

F32 = mybir.dt.float32
BF16 = mybir.dt.bfloat16
I32 = mybir.dt.int32
AF = mybir.ActivationFunctionType
ALU = mybir.AluOpType
AX = mybir.AxisListType

S, D, NH, HD, NKV = 8192, 4096, 32, 128, 8
KV = NKV * HD
NC, TPC = 8, 1024
ISCALE = 0.08838834764831845
NEXP, RANK = 8, 16
ER = NEXP * RANK
LN_EPS = 1e-5

DBG = ()
PHASES = 4
SUB1 = 3
ATT_H = 4
ATT_QC = 16
SKIP = 0



def stored_tokens():
    s = np.arange(S)
    return 256 * (s // 256) + 2 * (s % 128) + (s % 256) // 128


def core_stored_rows(c):
    return np.concatenate([np.arange(512 * c, 512 * c + 512),
                           np.arange(512 * (15 - c), 512 * (16 - c))])


CORE_HEADS = [[2 * c, 2 * c + 1, 30 - 2 * c, 31 - 2 * c] for c in range(NC)]
CORE_GROUPS = [[c // 2, (15 - c) // 2] for c in range(NC)]


def superblocks(c):
    return [(0, kc) for kc in range(c + 1)] + [(1, kc) for kc in range(16 - c)]


def chunk_owner(kc):
    return (kc, 0) if kc < 8 else (15 - kc, 512)


def build_bmasks(c):
    incl = (np.arange(128)[None, :] <= np.arange(128)[:, None]).astype(np.float32)
    strict = (np.arange(128)[None, :] <= (np.arange(128)[:, None] - 1)).astype(np.float32)
    diag = np.zeros((4, 128, 512), np.float32)
    for qt in range(4):
        for kt in range(4):
            if kt // 2 < qt // 2:
                blk = np.ones((128, 128), np.float32)
            elif kt // 2 > qt // 2:
                blk = np.zeros((128, 128), np.float32)
            else:
                blk = strict if (qt % 2 == 0 and kt % 2 == 1) else incl
            diag[qt, :, kt * 128:(kt + 1) * 128] = blk
    diagT = np.zeros((128, 2048), np.float32)
    for kt in range(4):
        for qt in range(4):
            diagT[:, kt * 512 + qt * 128:kt * 512 + (qt + 1) * 128] = \
                diag[qt][:, kt * 128:(kt + 1) * 128].T
    out = np.zeros((16, 128, 2048), np.float32)
    for kc in range(16):
        chunk = c if kc < 8 else 15 - c
        if kc < chunk:
            out[kc] = 1.0
        elif kc == chunk:
            out[kc] = diagT
    return np.ascontiguousarray(out.reshape(16 * 128, 2048)).astype(ml_dtypes.bfloat16)


def host_rope(tok):
    inv = (1.0 / (10000.0 ** (np.arange(0, HD, 2, dtype=np.float32) / HD))).astype(np.float32)
    fr = tok.astype(np.float32)[:, None] * inv[None, :]
    emb = np.concatenate([fr, fr], axis=1).astype(np.float32)
    cos, sin = np.cos(emb).astype(np.float32).T, np.sin(emb).astype(np.float32).T
    return (np.ascontiguousarray(cos * np.float32(ISCALE)),
            np.ascontiguousarray(sin * np.float32(ISCALE)),
            np.ascontiguousarray(cos), np.ascontiguousarray(sin))


def host_rot():
    M = np.zeros((128, 128), np.float32)
    for d in range(64):
        M[d + 64, d] = -1.0
        M[d, d + 64] = 1.0
    return M



ENGS = ("tensor", "scalar", "vector", "gpsimd", "sync")
SEMS = ("pe", "act", "dve", "pool", "cc", "st")
NSLOTS = 88


class KB:
    def __init__(self, nc):
        self.nc = nc
        self.stream = {e: [] for e in ENGS}
        self.cnt = {s: 0 for s in SEMS}
        self.sems = {}
        self.slotmap = {}
        self.waited = {e: {} for e in ENGS}
        self.regnames = {e: [] for e in ENGS}
        self.regs = {e: {} for e in ENGS}

    def tok(self, name):
        return (name, self.cnt[name])

    def slot(self, name):
        if name not in self.slotmap:
            i = len(self.slotmap)
            assert i < NSLOTS, "out of dma slots"
            self.slotmap[name] = f"q{i}"
            self.cnt.setdefault(f"q{i}", 0)
        return self.slotmap[name]

    def snap(self):
        return [self.tok(s) for s in list(self.cnt)]

    def op(self, eng, build, deps=(), inc=None, inck=1):
        w = self.waited[eng]
        need = {}
        for d in deps:
            if d is None:
                continue
            s, v = d
            if v > need.get(s, 0):
                need[s] = v
        use = []
        for s, v in need.items():
            if v <= 0 or w.get(s, 0) >= v:
                continue
            w[s] = v
            use.append((s, v))
        if inc is not None:
            self.cnt[inc] += inck

        def fn(e, kb=self, use=use, inc=inc, inck=inck, build=build):
            for (s, v) in use:
                e.wait_ge(kb.sems[s], v)
            ins = build(e, kb.regs)
            if inc is not None:
                ins.then_inc(kb.sems[inc], inck)
        self.stream[eng].append(fn)
        return (inc, self.cnt[inc]) if inc is not None else None

    def fence(self, engines=ENGS):
        sn = self.snap()
        for eng in engines:
            self.op(eng, lambda e, r: e.nop(), deps=sn)

    def reg(self, eng, name):
        if name not in self.regnames[eng]:
            self.regnames[eng].append(name)

    def emit(self, block):
        for eng in ENGS:
            ops = self.stream[eng]
            if not ops:
                continue

            def section(e, ops=ops, eng=eng):
                for nm in self.regnames[eng]:
                    self.regs[eng][nm] = e.alloc_register(nm)
                for f in ops:
                    f(e)
            getattr(block, eng)(section)


def dma(kb, q, dst, src, deps=(), slot="misc", sem=None):
    sem = sem or kb.slot(slot)
    return kb.op(q, lambda e, r: e.dma_start(out=dst, in_=src), deps=deps, inc=sem, inck=16)


class P:
    def g(self, key):
        return self.__dict__.get(key)


def r3(ap, p=128):
    return ap.rearrange("(a p) n -> p a n", p=p)


def s3(ap, a):
    return ap.rearrange("p (a n) -> p a n", a=a)




def build_program(dbg=(), phases=None):
    global PHASES
    if phases is not None:
        PHASES = phases
    nc = bass.Bass("TRN2", target_bir_lowering=False)
    kb = KB(nc)
    p = P()
    p.dbg = set(dbg)

    def din(name, shape, dt):
        setattr(p, name, nc.dram_tensor(name, list(shape), dt, kind="ExternalInput"))

    din("x", (TPC, D), F32)
    din("wq_a", (D, 512), F32); din("wk_a", (D, 256), F32); din("wv_a", (D, 256), F32)
    din("bq_a", (128, 4), F32); din("bk_a", (128, 2), F32); din("bv_a", (128, 2), F32)
    din("wo_a", (D, D), F32); din("bo_a", (1, D), F32)
    din("wbT", (D, D), F32); din("bb", (1, D), F32); din("wrT", (D, 8), F32)
    din("acT", (D, ER), F32); din("bcat", (ER, D), F32)
    din("qwq", (D, D), I32); din("qwk", (D, KV), I32); din("qwv", (D, KV), I32); din("qwo", (D, D), I32)
    din("scq", (32, D), F32); din("sck", (32, KV), F32); din("scv", (32, KV), F32); din("sco", (32, D), F32)
    din("ident", (128, 128), BF16); din("identf", (128, 128), F32); din("onesr", (1, 512), BF16); din("onesc", (128, 1), BF16)
    din("emat", (8, 128), BF16); din("rotm", (128, 128), BF16)
    din("ropeqc", (128, TPC), F32); din("ropeqs", (128, TPC), F32)
    din("ropekc", (128, TPC), F32); din("ropeks", (128, TPC), F32)
    din("bmasks", (16 * 128, 2048), BF16)
    din("onesrf", (1, 128), F32)

    p.out = nc.dram_tensor("out", [TPC, D], F32, kind="ExternalOutput")
    p.dbg_t = {}
    for name, shape in (("d_x1", (TPC, D)), ("d_x2", (TPC, D)), ("d_x3", (TPC, D)),
                        ("d_ln1ag", (NC * D, TPC)), ("d_qa", (512, S)), ("d_ka", (256, S)),
                        ("d_va", (S, 256)), ("d_oa", (4 * S, HD)), ("d_qb", (32 * TPC, HD)),
                        ("d_kag", (NC * KV, TPC)), ("d_vag", (NC * TPC, KV)),
                        ("d_wdq", (D, D)), ("d_wdo", (D, D)), ("d_ctx", (128, 32 * TPC)),
                        ("d_bound", (128, 256)), ("d_lsum", (128, 256)), ("d_kmax", (1, 8)),
                        ("d_hT", (128, 1024)), ("d_wexp", (128, 1024))):
        if name in p.dbg:
            p.dbg_t[name] = nc.dram_tensor(name, list(shape), F32, kind="ExternalOutput")

    p.xw = nc.dram_tensor("xw", [TPC, D], F32)
    p.ag1_in = nc.dram_tensor("ag1_in", [D, TPC], BF16)
    p.ag1_out = nc.dram_tensor("ag1_out", [NC * D, TPC], BF16)
    p.qa_d = nc.dram_tensor("qa_d", [512, S], BF16)
    p.ka_d = nc.dram_tensor("ka_d", [256, S], BF16)
    p.va_d = nc.dram_tensor("va_d", [S, 256], BF16)
    p.oa_d = nc.dram_tensor("oa_d", [4 * S, HD], BF16)
    p.qb_d = p.oa_d
    p.kb_in = nc.dram_tensor("kb_in", [KV, TPC], BF16)
    p.vb_in = nc.dram_tensor("vb_in", [TPC, KV], BF16)
    p.kb_ag = nc.dram_tensor("kb_ag", [NC * KV, TPC], BF16)
    p.vb_ag = nc.dram_tensor("vb_ag", [NC * TPC, KV], BF16)
    p.km_in = nc.dram_tensor("km_in", [1, 8], F32)
    p.km_out = nc.dram_tensor("km_out", [1, 8], F32)
    p.browd = nc.dram_tensor("browd", [64, 512], BF16)
    p.wdq_d = nc.dram_tensor("wdq_d", [D, D], BF16)
    p.wdk_d = nc.dram_tensor("wdk_d", [D, KV], BF16)
    p.wdv_d = nc.dram_tensor("wdv_d", [D, KV], BF16)
    p.wdo_d = nc.dram_tensor("wdo_d", [D, D], BF16)

    with ExitStack() as ctx:
        e = ctx.enter_context
        p.c_ident = e(nc.sbuf_tensor("c_ident", [128, 128], BF16))
        p.c_identf = e(nc.sbuf_tensor("c_identf", [128, 128], F32))
        p.c_onesr = e(nc.sbuf_tensor("c_onesr", [1, 512], BF16))
        p.c_onesc = e(nc.sbuf_tensor("c_onesc", [128, 1], BF16))
        p.c_onesrf = e(nc.sbuf_tensor("c_onesrf", [1, 128], F32))
        p.brst = [e(nc.sbuf_tensor(f"brst{i}", [1, 512], BF16)) for i in range(4)]
        p.c_emat = e(nc.sbuf_tensor("c_emat", [8, 128], BF16))
        p.c_rot = e(nc.sbuf_tensor("c_rot", [128, 128], BF16))
        p.c_bq = e(nc.sbuf_tensor("c_bq", [128, 4], F32))
        p.c_bk = e(nc.sbuf_tensor("c_bk", [128, 2], F32))
        p.c_bv = e(nc.sbuf_tensor("c_bv", [128, 2], F32))
        p.c_wrT = e(nc.sbuf_tensor("c_wrT", [128, 32 * 8], BF16))
        p.c_bcat = e(nc.sbuf_tensor("c_bcat", [128, D], BF16))
        p.A1 = e(nc.sbuf_tensor("A1", [128, 32768], BF16))
        p.W1 = e(nc.sbuf_tensor("W1", [128, 16384], BF16))
        p.W2 = e(nc.sbuf_tensor("W2", [128, 16384], BF16))
        p.xt = e(nc.sbuf_tensor("xt", [128, D], F32))
        p.lntok = e(nc.sbuf_tensor("lntok", [128, D], BF16))
        p.p_sb = e(nc.sbuf_tensor("p_sb", [128, 2048], BF16))
        p.pm_sb = e(nc.sbuf_tensor("pm_sb", [128, 2048], BF16))
        p.pt_sb = e(nc.sbuf_tensor("pt_sb", [128, 2048], BF16))
        p.stat = e(nc.sbuf_tensor("stat", [128, 16], F32))
        p.bnd = e(nc.sbuf_tensor("bnd", [128, 256], F32))
        p.lstg = e(nc.sbuf_tensor("lstg", [128, 16], F32))
        p.kmx = e(nc.sbuf_tensor("kmx", [128, 16], F32))
        p.row = e(nc.sbuf_tensor("row", [1, 1024], F32))
        p.rowb = e(nc.sbuf_tensor("rowb", [1, 512], BF16))
        p.brow = e(nc.sbuf_tensor("brow", [1, 1024], BF16))
        p.wexp = e(nc.sbuf_tensor("wexp", [128, 1024], BF16))
        p.hT = e(nc.sbuf_tensor("hT", [128, 1024], BF16))
        p.wrow = e(nc.sbuf_tensor("wrow", [8, 1024], BF16))
        p.evs = e(nc.sbuf_tensor("evs", [128, 1024], F32))
        p.evs2 = e(nc.sbuf_tensor("evs2", [128, 1024], F32))
        p.psF = [e(nc.psum_tensor(f"psF{i}", [128, 512], F32)) for i in range(6)]
        p.psB = [e(nc.psum_tensor(f"psB{i}", [128, 1024], BF16)) for i in range(2)]
        for s in SEMS:
            kb.sems[s] = e(nc.semaphore(s))
        for i in range(NSLOTS):
            kb.sems[f"q{i}"] = e(nc.semaphore(f"q{i}"))

        phase_setup(kb, p)
        phase_sub1(kb, p)
        if PHASES >= 2:
            phase_ffn(kb, p, 1)
        if PHASES >= 3:
            phase_sub3(kb, p)
        if PHASES >= 4:
            phase_ffn(kb, p, 3)
        if PHASES < 4:
            kb.fence()
            dma(kb, "gpsimd", p.out.ap(), p.xw.ap(), deps=kb.snap(), sem="st")
        fin = kb.snap()
        kb.op("gpsimd", lambda e, r: e.nop(), deps=fin)
        block = e(nc.Block())
        kb.emit(block)
    return nc


def phase_setup(kb, p):
    for dst, src in [(p.c_ident, p.ident), (p.c_identf, p.identf),
                     (p.c_onesr, p.onesr), (p.c_onesc, p.onesc),
                     (p.c_onesrf, p.onesrf),
                     (p.c_emat, p.emat), (p.c_rot, p.rotm),
                     (p.c_bq, p.bq_a), (p.c_bk, p.bk_a), (p.c_bv, p.bv_a)]:
        dma(kb, "sync", dst[:, :], src.ap(), slot="cst")
    dma(kb, "gpsimd", p.c_bcat[:, :], p.bcat.ap(), slot="cst2")
    dma(kb, "gpsimd", s3(p.c_wrT[:, :], 32), r3(p.wrT.ap()), slot="cst2")
    p.setup = [kb.tok(kb.slot("cst")), kb.tok(kb.slot("cst2"))]




def emit_ln(kb, p, src_dram, extra_deps=()):
    kb.fence()
    deps0 = list(p.setup) + list(extra_deps)
    xt_busy = None
    fscr_busy = None
    psb_busy = None
    for tt in range(8):
        ld = dma(kb, "sync", p.xt[:, :], src_dram.ap()[tt * 128:(tt + 1) * 128, :],
                 deps0 + [xt_busy], slot="xt")
        s1 = kb.op("vector", lambda e, r: e.reduce_sum(p.stat[:, 0:1], p.xt[:, :], axis=AX.X),
                   deps=[ld], inc="dve")
        W2sc = p.W2.ap().bitcast(F32)
        sq = kb.op("scalar", lambda e, r, W2sc=W2sc: e.activation(W2sc[:, 0:4096], p.xt[:, :], AF.Square),
                   deps=[ld, fscr_busy], inc="act")
        s2 = kb.op("vector", lambda e, r, W2sc=W2sc: e.reduce_sum(p.stat[:, 1:2], W2sc[:, 0:4096], axis=AX.X),
                   deps=[sq, s1], inc="dve")
        fscr_busy = s2
        mv = kb.op("vector", lambda e, r: e.tensor_scalar(p.stat[:, 2:4], p.stat[:, 0:2], 1.0 / D, None,
                                                          ALU.mult), deps=[s2], inc="dve")
        m2 = kb.op("vector", lambda e, r: e.tensor_tensor(p.stat[:, 4:5], p.stat[:, 2:3], p.stat[:, 2:3],
                                                          ALU.mult), deps=[mv], inc="dve")
        vv = kb.op("vector", lambda e, r: e.tensor_tensor(p.stat[:, 5:6], p.stat[:, 3:4], p.stat[:, 4:5],
                                                          ALU.subtract), deps=[m2], inc="dve")
        ve = kb.op("vector", lambda e, r: e.tensor_scalar(p.stat[:, 8:9], p.stat[:, 5:6],
                                                          float(LN_EPS), None, ALU.add),
                   deps=[vv], inc="dve")
        l1 = kb.op("scalar", lambda e, r: e.activation(p.stat[:, 6:7], p.stat[:, 8:9], AF.Ln),
                   deps=[ve], inc="act")
        l2 = kb.op("scalar", lambda e, r: e.activation(p.stat[:, 7:8], p.stat[:, 6:7], AF.Exp, scale=-0.5),
                   deps=[l1], inc="act")
        nrm = kb.op("vector", lambda e, r: e.tensor_scalar(p.lntok[:, :], p.xt[:, :], p.stat[:, 2:3],
                                                           p.stat[:, 7:8], ALU.subtract, ALU.mult),
                    deps=[l2], inc="dve")
        xt_busy = nrm
        for d4 in range(8):
            trs = []
            for k in range(4):
                trs.append(kb.op("tensor", lambda e, r, d4=d4, k=k: e.transpose(
                    p.psB[0][:, k * 128:(k + 1) * 128],
                    p.lntok[:, (d4 * 4 + k) * 128:(d4 * 4 + k + 1) * 128], p.c_ident[:, :]),
                    deps=[nrm, psb_busy], inc=("pe" if k == 3 else None)))
            cp = kb.op("vector", lambda e, r, d4=d4, tt=tt: e.tensor_copy(
                p.A1.ap().rearrange("p (dm t) -> p dm t", dm=32)[:, d4 * 4:d4 * 4 + 4,
                                                                tt * 128:(tt + 1) * 128],
                s3(p.psB[0][:, 0:512], 4)), deps=[trs[-1]], inc="dve")
            psb_busy = cp
    return [kb.tok("dve"), kb.tok("pe")]




def phase_sub1(kb, p):
    ln_done = emit_ln(kb, p, p.x)
    t = dma(kb, "gpsimd", r3(p.ag1_in.ap()), s3(p.A1[:, :], 32), deps=ln_done, slot="ag1s")
    ag = kb.op("gpsimd", lambda e, r: e.collective_compute(
        "AllGather", ALU.bypass, replica_groups=[list(range(NC))],
        ins=[p.ag1_in.ap()], outs=[p.ag1_out.ap()]), deps=[t], inc="cc")
    p.ag1 = ag
    if "d_ln1ag" in p.dbg_t:
        dump_dram(kb, p, p.dbg_t["d_ln1ag"], p.ag1_out, [ag])
    if SUB1 >= 1:
        attn_a_proj(kb, p)
    if SUB1 >= 2:
        attn_a_attend(kb, p)
    if SUB1 >= 3:
        attn_a_wo(kb, p)
    else:
        p.sub1_done = kb.snap()
    if "d_x1" in p.dbg_t:
        dump_dram(kb, p, p.dbg_t["d_x1"], p.xw, kb.snap())


def attn_a_proj(kb, p):
    kb.fence()
    wq_off = 0
    wk_off, wv_off = 0, 32 * 256
    wld = [dma(kb, "gpsimd", s3(p.W1[:, 0:32 * 512], 32), r3(p.wq_a.ap()), deps=p.setup, slot="pwq"),
           dma(kb, "gpsimd", s3(p.W2[:, wk_off:wk_off + 32 * 256], 32), r3(p.wk_a.ap()), deps=p.setup,
               slot="pwkv"),
           dma(kb, "gpsimd", s3(p.W2[:, wv_off:wv_off + 32 * 256], 32), r3(p.wv_a.ap()), deps=p.setup,
               slot="pwkv")]
    rhs_cons = [None, None]
    ev_busy = {}
    stg_busy = {"pq": None, "v": None}
    for tc in range(16):
        half = tc % 2
        r, colb = chunk_owner(tc)
        src = r3(p.ag1_out.ap()[D * r:D * (r + 1), colb:colb + 512])
        ld = dma(kb, "sync", s3(p.A1[:, half * 16384:(half + 1) * 16384], 32), src,
                 deps=[p.ag1, rhs_cons[half]] + wld, slot=f"prh{half}")
        for kind, nt, W, woff, wstr, bcol, sc, bank0 in (
                ("q", 4, p.W1, wq_off, 512, p.c_bq, float(ISCALE), 0),
                ("k", 2, p.W2, wk_off, 256, p.c_bk, 1.0, 1),
                ("v", 2, p.W2, wv_off, 256, p.c_bv, 1.0, 2)):
            for ot in range(nt):
                bank = bank0 + 3 * (ot % 2)
                mm = None
                for dm in range(32):
                    mm = kb.op("tensor", lambda e, r_, dm=dm, ot=ot, woff=woff, wstr=wstr, W=W, half=half,
                               bank=bank: e.matmul(
                        p.psF[bank][:, :],
                        W[:, woff + dm * wstr + ot * 128:woff + dm * wstr + ot * 128 + 128],
                        p.A1[:, half * 16384 + dm * 512:half * 16384 + (dm + 1) * 512],
                        start=(dm == 0), stop=(dm == 31)),
                        deps=([ld, ev_busy.get(bank)] if dm == 0 else []),
                        inc=("pe" if dm == 31 else None))
                if kind == "v":
                    ev = kb.op("vector", lambda e, r_, bank=bank, bcol=bcol, ot=ot: e.tensor_scalar(
                        p.pm_sb[:, 0:512], p.psF[bank][:, :], bcol[:, ot:ot + 1], None, ALU.add),
                        deps=[mm, stg_busy["v"]], inc="dve")
                    ev_busy[bank] = ev
                    trs = []
                    for k in range(4):
                        trs.append(kb.op("tensor", lambda e, r_, k=k: e.transpose(
                            p.psB[1][:, k * 128:(k + 1) * 128], p.pm_sb[:, k * 128:(k + 1) * 128],
                            p.c_ident[:, :]), deps=[ev, p.g("avcp2")], inc=("pe" if k == 3 else None)))
                    cp = kb.op("vector", lambda e, r_: e.tensor_copy(p.pt_sb[:, 0:512], p.psB[1][:, 0:512]),
                               deps=[trs[-1], p.g("avst")], inc="dve")
                    p.avcp2 = cp
                    st = dma(kb, "gpsimd",
                             r3(p.va_d.ap()[tc * 512:(tc + 1) * 512, ot * 128:(ot + 1) * 128]),
                             s3(p.pt_sb[:, 0:512], 4), deps=[cp], slot="pvst")
                    p.avst = st
                    stg_busy["v"] = st
                else:
                    if sc == 1.0:
                        ev = kb.op("vector", lambda e, r_, bank=bank, bcol=bcol, ot=ot: e.tensor_scalar(
                            p.pt_sb[:, 512:1024], p.psF[bank][:, :], bcol[:, ot:ot + 1], None,
                            ALU.add), deps=[mm, stg_busy["pq"]], inc="dve")
                    else:
                        ev = kb.op("vector", lambda e, r_, bank=bank, bcol=bcol, ot=ot, sc=sc: e.tensor_scalar(
                            p.pt_sb[:, 512:1024], p.psF[bank][:, :], bcol[:, ot:ot + 1], sc,
                            ALU.add, ALU.mult), deps=[mm, stg_busy["pq"]], inc="dve")
                    ev_busy[bank] = ev
                    outd = p.qa_d if kind == "q" else p.ka_d
                    st = dma(kb, "sync", outd.ap()[ot * 128:(ot + 1) * 128, tc * 512:(tc + 1) * 512],
                             p.pt_sb[:, 512:1024], deps=[ev], slot="pqst")
                    stg_busy["pq"] = st
        rhs_cons[half] = kb.tok("pe")
    p.aproj_done = kb.snap()
    for nm, d in (("d_qa", p.qa_d), ("d_ka", p.ka_d), ("d_va", p.va_d)):
        if nm in p.dbg_t:
            dump_dram(kb, p, p.dbg_t[nm], d, p.aproj_done)


def attn_a_attend(kb, p):
    kb.fence()
    if PHASES >= 3:
        deq_prep(kb, p)
    KOFF = [0, 8192]
    VOFF = [16384, 24576]
    kvtok = {}
    qcons = [None, None]
    exp_busy = [None] * 3
    for hi in range(ATT_H):
        gi = 0 if hi < 2 else 1
        qh = hi % 2
        qld = dma(kb, "sync", p.W1[:, qh * 8192:(qh + 1) * 8192], p.qa_d.ap()[hi * 128:(hi + 1) * 128, :],
                  deps=p.aproj_done + [qcons[qh]], slot=f"aq{qh}")
        if hi in (0, 2):
            kld = dma(kb, "sync", p.A1[:, KOFF[gi]:KOFF[gi] + 8192],
                      p.ka_d.ap()[gi * 128:(gi + 1) * 128, :], deps=p.aproj_done, slot=f"ak{gi}")
            if SKIP & 1:
                vld = dma(kb, "sync", s3(p.A1[:, VOFF[gi]:VOFF[gi] + 8192], 64),
                          r3(p.va_d.ap()[:, gi * 128:(gi + 1) * 128]), deps=p.aproj_done,
                          slot=f"av{gi}")
                k2 = kb.op("vector", lambda e, r, gi=gi: e.memset(p.kmx[:, gi:gi + 1], 25.0),
                           deps=[], inc="dve")
                p.akmcp = k2
                p.abnd_sr = k2
                kvtok[gi] = (kld, vld, k2)
                kld, vld, k2 = kvtok[gi]
                do_kmax = False
            else:
                do_kmax = True
        else:
            do_kmax = False
        if do_kmax:
            vld = dma(kb, "sync", s3(p.A1[:, VOFF[gi]:VOFF[gi] + 8192], 64),
                      r3(p.va_d.ap()[:, gi * 128:(gi + 1) * 128]), deps=p.aproj_done, slot=f"av{gi}")
            fb = p.W2
            sqs = []
            for i in range(4):
                sqs.append(kb.op("scalar", lambda e, r, i=i, gi=gi: e.activation(
                    fb[:, i * 2048:(i + 1) * 2048], p.A1[:, KOFF[gi] + i * 2048:KOFF[gi] + (i + 1) * 2048],
                    AF.Square), deps=[kld, p.g("akmsq"), p.g("akmnm")], inc="act"))
            p.akmsq = sqs[-1]
            nmm = None
            for kt in range(64):
                nmm = kb.op("tensor", lambda e, r, kt=kt: e.matmul(
                    p.psF[3][:, kt:kt + 1], fb[:, kt * 128:(kt + 1) * 128], p.c_onesc[:, :],
                    start=True, stop=True), deps=[sqs[kt // 16], p.g("akmcp"), p.g("aoev3")],
                    inc=("pe" if kt == 63 else None))
            p.akmnm = nmm
            cp = kb.op("vector", lambda e, r: e.tensor_copy(p.evs[:, 0:64], p.psF[3][:, 0:64]),
                       deps=[nmm], inc="dve")
            mx = kb.op("vector", lambda e, r: e.reduce_max(p.kmx[:, 8:9], p.evs[:, 0:64], axis=AX.X),
                       deps=[cp], inc="dve")
            tr = kb.op("tensor", lambda e, r: e.transpose(p.psF[3][0:1, 64:192], p.kmx[:, 8:9],
                                                          p.c_identf[:, :]), deps=[mx], inc="pe")
            cp2 = kb.op("vector", lambda e, r: e.tensor_copy(p.row[0:1, 0:128], p.psF[3][0:1, 64:192]),
                        deps=[tr], inc="dve")
            mx2 = kb.op("vector", lambda e, r: e.reduce_max(p.row[0:1, 128:129], p.row[0:1, 0:128],
                                                            axis=AX.X), deps=[cp2], inc="dve")
            cb = kb.op("vector", lambda e, r: e.tensor_copy(p.rowb[0:1, 0:1], p.row[0:1, 128:129]),
                       deps=[mx2], inc="dve")
            bc = kb.op("tensor", lambda e, r: e.matmul(p.psF[3][:, 192:193], p.c_onesr[0:1, 0:128],
                                                       p.rowb[0:1, 0:1], start=True, stop=True),
                       deps=[cb], inc="pe")
            k2 = kb.op("vector", lambda e, r, gi=gi: e.tensor_copy(p.kmx[:, gi:gi + 1],
                                                                   p.psF[3][:, 192:193]),
                       deps=[bc], inc="dve")
            p.akmcp = k2
            p.abnd_sr = k2
            kvtok[gi] = (kld, vld, k2)
        kld, vld, k2 = kvtok[gi]

        for qc in range(ATT_QC):
            if PHASES >= 3:
                deq_step(kb, p, 5)
            sq = kb.op("scalar", lambda e, r, qh=qh, qc=qc: e.activation(
                p.pm_sb[:, 0:512], p.W1[:, qh * 8192 + qc * 512:qh * 8192 + (qc + 1) * 512],
                AF.Square), deps=[qld, p.g("abnd_sr")], inc="act")
            rmm = kb.op("tensor", lambda e, r: e.matmul(
                p.psF[3][0:1, 0:512], p.c_onesc[:, :], p.pm_sb[:, 0:512],
                start=True, stop=True),
                deps=[sq, p.g("abnd_sr"), p.g("aoev3"), p.g("akmcp")], inc="pe")
            srow = kb.op("scalar", lambda e, r, gi=gi, qc=qc: e.activation(
                p.brst[qc % 2][0:1, :], p.psF[3][0:1, 0:512], AF.Sqrt,
                scale=p.kmx[0:1, gi:gi + 1]),
                deps=[rmm, k2, p.g(f"abrc{qc % 2}")], inc="act")
            p.abnd_sr = srow
            neg = kb.op("vector", lambda e, r, qc=qc: e.tensor_scalar(
                p.brst[qc % 2][0:1, :], p.brst[qc % 2][0:1, :], -1.0, None, ALU.mult),
                deps=[srow], inc="dve")
            pvt = None
            for kc in range(16):
                for kt in range(4):
                    bank = kt % 3
                    kb.op("tensor", lambda e, r, bank=bank, qc=qc: e.matmul(
                        p.psF[bank][:, :], p.c_onesr[0:1, 0:128], p.brst[qc % 2][0:1, :],
                        start=True, stop=False), deps=[exp_busy[bank], neg], inc=None)
                    smm = kb.op("tensor", lambda e, r, bank=bank, kt=kt, kc=kc, qh=qh, qc=qc,
                                gi=gi: e.matmul(
                        p.psF[bank][:, :],
                        p.A1[:, KOFF[gi] + kc * 512 + kt * 128:KOFF[gi] + kc * 512 + (kt + 1) * 128],
                        p.W1[:, qh * 8192 + qc * 512:qh * 8192 + (qc + 1) * 512],
                        start=False, stop=True), deps=[kld, qld], inc="pe")
                    ex = kb.op("scalar", lambda e, r, bank=bank, kt=kt: e.activation(
                        p.p_sb[:, kt * 512:(kt + 1) * 512], p.psF[bank][:, :], AF.Exp),
                        deps=[smm], inc="act")
                    exp_busy[bank] = ex
                    first = (kc == 0 and kt == 0)
                    last = (kc == 15 and kt == 3)
                    kb.op("tensor", lambda e, r, kt=kt, first=first, last=last: e.matmul(
                        p.psF[5][0:1, 0:512], p.c_onesc[:, :],
                        p.p_sb[:, kt * 512:(kt + 1) * 512],
                        start=first, stop=last),
                        deps=[ex] + ([p.g("alrc")] if first else []), inc=None)
                    pvt = kb.op("tensor", lambda e, r, kt=kt, kc=kc, gi=gi, first=first,
                                last=last: e.matmul(
                        p.psF[4][:, :],
                        p.A1[:, VOFF[gi] + (kc * 4 + kt) * 128:VOFF[gi] + (kc * 4 + kt + 1) * 128],
                        p.p_sb[:, kt * 512:(kt + 1) * 512],
                        start=first, stop=last),
                        deps=[ex, vld] + ([p.g("aoev")] if first else []),
                        inc=("pe" if last else None))
            p.__dict__[f"abrc{qc % 2}"] = pvt
            rc = kb.op("vector", lambda e, r: e.reciprocal(p.evs[0:1, 0:512],
                                                           p.psF[5][0:1, 0:512]),
                       deps=[pvt], inc="dve")
            p.alrc = rc
            bcst = kb.op("tensor", lambda e, r: e.matmul(
                p.psF[3][:, :], p.c_onesrf[0:1, 0:128], p.evs[0:1, 0:512],
                start=True, stop=True), deps=[rc, p.g("abnd_sr"), p.g("aoev3")], inc="pe")
            oev = kb.op("vector", lambda e, r: e.tensor_copy(p.pm_sb.ap()[:, 1024:1536],
                                                             p.psF[4][:, :]),
                        deps=[pvt, p.g("aepi_tr")], inc="dve")
            p.aoev = oev
            osm = kb.op("vector", lambda e, r: e.tensor_tensor(
                p.pm_sb[:, 1024:1536], p.pm_sb[:, 1024:1536], p.psF[3][:, :], ALU.mult),
                deps=[oev, bcst], inc="dve")
            p.aoev3 = osm
            otr = None
            for qt in range(4):
                otr = kb.op("tensor", lambda e, r, qt=qt: e.transpose(
                    p.psB[0][:, qt * 128:(qt + 1) * 128],
                    p.pm_sb[:, 1024 + qt * 128:1024 + (qt + 1) * 128], p.c_ident[:, :]),
                    deps=[osm, p.g("apcb_0")],
                    inc=("pe" if qt == 3 else None))
            p.aepi_tr = otr
            ocp = kb.op("vector", lambda e, r: e.tensor_copy(p.pm_sb[:, 1536:2048],
                                                             p.psB[0][:, 0:512]),
                        deps=[otr, p.g("aost")], inc="dve")
            p.apcb_0 = ocp
            st = dma(kb, "gpsimd",
                     r3(p.oa_d.ap()[hi * 8192 + qc * 512:hi * 8192 + (qc + 1) * 512, :]),
                     s3(p.pm_sb[:, 1536:2048], 4), deps=[ocp], slot="aost")
            p.aost = st
            if PHASES >= 3:
                deq_step(kb, p, 5)
        qcons[qh] = kb.tok("pe")
    if PHASES >= 3:
        deq_step(kb, p, 0)
    p.aatt_done = kb.snap()
    if "d_oa" in p.dbg_t:
        dump_dram(kb, p, p.dbg_t["d_oa"], p.oa_d, p.aatt_done)


def attn_a_wo(kb, p):
    kb.fence()
    old = None
    for i in range(4):
        old = dma(kb, "gpsimd", s3(p.A1[:, i * 8192:(i + 1) * 8192], 64),
                  r3(p.oa_d.ap()[i * 8192:(i + 1) * 8192, :]), deps=p.aatt_done, slot="wold")
    wcons = [None, None]
    ev_busy = [None, None]
    xst = {}
    for oc in range(8):
        half = oc % 2
        W = p.W1 if half == 0 else p.W2
        wld = dma(kb, "gpsimd", s3(W[:, 0:32 * 512], 32),
                  r3(p.wo_a.ap()[:, oc * 512:(oc + 1) * 512]),
                  deps=list(p.setup) + [wcons[half]] + (p.aatt_done if oc < 2 else []), slot=f"wow{half}")
        p.__dict__[f"brld_{half}"] = dma(kb, "gpsimd", p.brow[0:1, half * 512:half * 512 + 512],
                                         p.bo_a.ap()[0:1, oc * 512:(oc + 1) * 512],
                                         deps=[wcons[half]], slot=f"wob{half}")
        for hj in range(8):
            hi, j = hj // 2, hj % 2
            bank = hj % 2
            mm = None
            for qp in range(32):
                mm = kb.op("tensor", lambda e, r, hi=hi, j=j, qp=qp, W=W, bank=bank: e.matmul(
                    p.psF[bank][:, :],
                    p.A1[:, (hi * 64 + j * 32 + qp) * 128:(hi * 64 + j * 32 + qp + 1) * 128],
                    W[:, qp * 512:(qp + 1) * 512], start=(qp == 0), stop=False),
                    deps=([old, wld, ev_busy[bank]] if qp == 0 else []), inc=None)
            mm = kb.op("tensor", lambda e, r, oc=oc, bank=bank, half=half: e.matmul(
                p.psF[bank][:, :], p.c_onesr[0:1, 0:128], p.brow[0:1, half * 512:half * 512 + 512],
                start=False, stop=True), deps=[p.g(f"brld_{half}")], inc="pe")
            co = bank * 512
            xld = dma(kb, "sync", p.evs[:, co:co + 512],
                      p.x.ap()[hj * 128:(hj + 1) * 128, oc * 512:(oc + 1) * 512],
                      deps=[ev_busy[bank]], slot=f"wx{bank}")
            ev = kb.op("vector", lambda e, r, bank=bank, co=co: e.tensor_tensor(
                p.evs2[:, co:co + 512], p.psF[bank][:, :], p.evs[:, co:co + 512], ALU.add),
                deps=[mm, xld, ev_busy[bank], xst.get(bank)], inc="dve")
            ev_busy[bank] = ev
            st = dma(kb, "sync", p.xw.ap()[hj * 128:(hj + 1) * 128, oc * 512:(oc + 1) * 512],
                     p.evs2[:, co:co + 512], deps=[ev], slot=f"wxs{bank}")
            xst[bank] = st
        wcons[half] = kb.tok("pe")
    p.sub1_done = kb.snap()




def phase_ffn(kb, p, which):
    prev = p.sub1_done if which == 1 else p.sub3_done
    src = p.xw
    dst = p.xw if which == 1 else p.out
    ln_done = emit_ln(kb, p, src, extra_deps=prev)

    lg_busy = None
    w_done = []
    for tt in range(8):
        mm = None
        for dm in range(32):
            mm = kb.op("tensor", lambda e, r, dm=dm, tt=tt: e.matmul(
                p.psF[2][:, 0:8], p.A1[:, dm * 1024 + tt * 128:dm * 1024 + (tt + 1) * 128],
                p.c_wrT[:, dm * 8:(dm + 1) * 8], start=(dm == 0), stop=(dm == 31)),
                deps=(ln_done + [lg_busy] if dm == 0 else []), inc=("pe" if dm == 31 else None))
        lg = kb.op("vector", lambda e, r: e.tensor_copy(p.evs[:, 0:8], p.psF[2][:, 0:8]),
                   deps=[mm], inc="dve")
        m1 = kb.op("vector", lambda e, r: e.reduce_max(p.stat[:, 0:1], p.evs[:, 0:8], axis=AX.X),
                   deps=[lg], inc="dve")
        e1 = kb.op("vector", lambda e, r: e.tensor_scalar(p.evs[:, 8:16], p.evs[:, 0:8], p.stat[:, 0:1],
                                                          None, ALU.is_ge), deps=[m1], inc="dve")
        t2 = kb.op("vector", lambda e, r: e.scalar_tensor_tensor(p.evs[:, 16:24], p.evs[:, 8:16], -1e30,
                                                                 p.evs[:, 0:8], ALU.mult, ALU.add),
                   deps=[e1], inc="dve")
        m2 = kb.op("vector", lambda e, r: e.reduce_max(p.stat[:, 1:2], p.evs[:, 16:24], axis=AX.X),
                   deps=[t2], inc="dve")
        e2 = kb.op("vector", lambda e, r: e.tensor_scalar(p.evs[:, 24:32], p.evs[:, 16:24],
                                                          p.stat[:, 1:2], None, ALU.is_ge),
                   deps=[m2], inc="dve")
        df = kb.op("vector", lambda e, r: e.tensor_tensor(p.stat[:, 2:3], p.stat[:, 1:2], p.stat[:, 0:1],
                                                          ALU.subtract), deps=[e2], inc="dve")
        eb = kb.op("scalar", lambda e, r: e.activation(p.stat[:, 3:4], p.stat[:, 2:3], AF.Exp),
                   deps=[df], inc="act")
        dn = kb.op("vector", lambda e, r: e.tensor_scalar(p.stat[:, 4:5], p.stat[:, 3:4], 1.0, None,
                                                          ALU.add), deps=[eb], inc="dve")
        rcp = kb.op("vector", lambda e, r: e.reciprocal(p.stat[:, 5:6], p.stat[:, 4:5]),
                    deps=[dn], inc="dve")
        w1 = kb.op("vector", lambda e, r: e.tensor_scalar(p.stat[:, 6:7], p.stat[:, 5:6], 2.0, None,
                                                          ALU.mult), deps=[rcp], inc="dve")
        w2 = kb.op("vector", lambda e, r: e.tensor_tensor(p.stat[:, 7:8], p.stat[:, 3:4], p.stat[:, 6:7],
                                                          ALU.mult), deps=[w1], inc="dve")
        s1 = kb.op("vector", lambda e, r: e.tensor_scalar(p.evs[:, 8:16], p.evs[:, 8:16], p.stat[:, 6:7],
                                                          None, ALU.mult), deps=[w2], inc="dve")
        s2 = kb.op("vector", lambda e, r: e.tensor_scalar(p.evs[:, 24:32], p.evs[:, 24:32],
                                                          p.stat[:, 7:8], None, ALU.mult),
                   deps=[s1], inc="dve")
        wf = kb.op("vector", lambda e, r: e.tensor_tensor(p.pm_sb[:, 0:8], p.evs[:, 8:16],
                                                          p.evs[:, 24:32], ALU.add), deps=[s2], inc="dve")
        tr = kb.op("tensor", lambda e, r: e.transpose(p.psB[1][0:8, 0:128], p.pm_sb[:, 0:8],
                                                      p.c_ident[:, :]), deps=[wf, lg_busy], inc="pe")
        cp = kb.op("vector", lambda e, r, tt=tt: e.tensor_copy(p.wrow[0:8, tt * 128:(tt + 1) * 128],
                                                               p.psB[1][0:8, 0:128]), deps=[tr], inc="dve")
        lg_busy = cp
        w_done.append(cp)
    ex1 = kb.op("tensor", lambda e, r: e.matmul(p.psF[2][:, :], p.c_emat[:, :], p.wrow[0:8, 0:512],
                                                start=True, stop=True), deps=w_done, inc="pe")
    ex2 = kb.op("tensor", lambda e, r: e.matmul(p.psF[3][:, :], p.c_emat[:, :], p.wrow[0:8, 512:1024],
                                                start=True, stop=True), deps=[], inc="pe")
    wx1 = kb.op("vector", lambda e, r: e.tensor_copy(p.wexp[:, 0:512], p.psF[2][:, :]),
                deps=[ex1], inc="dve")
    wx2 = kb.op("vector", lambda e, r: e.tensor_copy(p.wexp[:, 512:1024], p.psF[3][:, :]),
                deps=[ex2], inc="dve")

    acld = dma(kb, "gpsimd", s3(p.W2[:, 0:4096], 32), r3(p.acT.ap()),
               deps=ln_done + list(p.setup), slot="fac")
    hT_done = []
    for tc in range(2):
        mm = None
        for dm in range(32):
            mm = kb.op("tensor", lambda e, r, dm=dm, tc=tc: e.matmul(
                p.psF[2 + tc][:, :], p.W2[:, dm * 128:(dm + 1) * 128],
                p.A1[:, dm * 1024 + tc * 512:dm * 1024 + (tc + 1) * 512],
                start=(dm == 0), stop=(dm == 31)),
                deps=([wx1, wx2, acld] if dm == 0 else []), inc=("pe" if dm == 31 else None))
        hw = kb.op("vector", lambda e, r, tc=tc: e.tensor_tensor(
            p.hT[:, tc * 512:(tc + 1) * 512], p.psF[2 + tc][:, :], p.wexp[:, tc * 512:(tc + 1) * 512],
            ALU.mult), deps=[mm], inc="dve")
        hT_done.append(hw)

    if which == 1 and "d_hT" in p.dbg_t:
        cpd = kb.op("vector", lambda e, r: e.tensor_copy(p.evs[:, 0:1024], p.hT[:, :]),
                    deps=hT_done, inc="dve")
        dma(kb, "gpsimd", p.dbg_t["d_hT"].ap(), p.evs[:, 0:1024], deps=[cpd], slot="dbg")
    if which == 1 and "d_wexp" in p.dbg_t:
        cpw = kb.op("vector", lambda e, r: e.tensor_copy(p.evs[:, 0:1024], p.wexp[:, :]),
                    deps=[wx1, wx2] + (hT_done if True else []), inc="dve")
        dma(kb, "gpsimd", p.dbg_t["d_wexp"].ap(), p.evs[:, 0:1024], deps=[cpw], slot="dbg")
    kb.fence(("sync", "vector", "tensor"))
    wcons = [None, None]
    ev_busy = [None, None]
    xst = {}
    semname = None
    for oc in range(8):
        half = oc % 2
        W = p.W1 if half == 0 else p.W2
        wld = dma(kb, "gpsimd", s3(W[:, 0:32 * 512], 32),
                  r3(p.wbT.ap()[:, oc * 512:(oc + 1) * 512]),
                  deps=list(p.setup) + [wcons[half]] + (prev if oc < 2 else []) +
                       (hT_done if oc < 2 else []), slot=f"fw{half}")
        p.__dict__[f"fbrld_{half}"] = dma(kb, "gpsimd", p.brow[0:1, half * 512:half * 512 + 512],
                                          p.bb.ap()[0:1, oc * 512:(oc + 1) * 512],
                                          deps=[wcons[half]] + (hT_done if oc < 2 else []),
                                          slot=f"fb{half}")
        for tt in range(8):
            bank = tt % 2
            mm = None
            for dm in range(32):
                mm = kb.op("tensor", lambda e, r, dm=dm, tt=tt, W=W, bank=bank: e.matmul(
                    p.psF[bank][:, :], p.A1[:, dm * 1024 + tt * 128:dm * 1024 + (tt + 1) * 128],
                    W[:, dm * 512:(dm + 1) * 512], start=(dm == 0), stop=False),
                    deps=([wld, ev_busy[bank]] + hT_done if dm == 0 else []), inc=None)
            kb.op("tensor", lambda e, r, oc=oc, bank=bank, half=half: e.matmul(
                p.psF[bank][:, :], p.c_onesr[0:1, 0:128], p.brow[0:1, half * 512:half * 512 + 512],
                start=False, stop=False), deps=[p.g(f"fbrld_{half}")], inc=None)
            mm = kb.op("tensor", lambda e, r, oc=oc, tt=tt, bank=bank: e.matmul(
                p.psF[bank][:, :], p.hT[:, tt * 128:(tt + 1) * 128],
                p.c_bcat[:, oc * 512:(oc + 1) * 512], start=False, stop=True), deps=[], inc="pe")
            co = bank * 512
            xld = dma(kb, "sync", p.evs[:, co:co + 512],
                      src.ap()[tt * 128:(tt + 1) * 128, oc * 512:(oc + 1) * 512],
                      deps=[ev_busy[bank]] + (prev if oc < 2 else []), slot=f"fx{bank}")
            ev = kb.op("vector", lambda e, r, bank=bank, co=co: e.tensor_tensor(
                p.evs2[:, co:co + 512], p.psF[bank][:, :], p.evs[:, co:co + 512], ALU.add),
                deps=[mm, xld, ev_busy[bank], xst.get(bank)], inc="dve")
            ev_busy[bank] = ev
            st = dma(kb, "sync", dst.ap()[tt * 128:(tt + 1) * 128, oc * 512:(oc + 1) * 512],
                     p.evs2[:, co:co + 512], deps=[ev],
                     slot=("fs3" if which == 3 else "fs1") + str(bank), sem=semname)
            xst[bank] = st
        wcons[half] = kb.tok("pe")
    done = kb.snap()
    if which == 1:
        p.ffn1_done = done
        if "d_x2" in p.dbg_t:
            dump_dram(kb, p, p.dbg_t["d_x2"], p.xw, done)




def phase_sub3(kb, p):
    deq_finish(kb, p)
    ln_done = emit_ln(kb, p, p.xw, extra_deps=p.ffn1_done + p.deq_done)
    attn_b_proj(kb, p, ln_done)
    attn_b_attend(kb, p)
    attn_b_oproj(kb, p)
    if "d_x3" in p.dbg_t:
        dump_dram(kb, p, p.dbg_t["d_x3"], p.xw, p.sub3_done)


def deq_prep(kb, p):
    p.dq_units = []
    for qw, sc, outd, N in ((p.qwq, p.scq, p.wdq_d, D), (p.qwo, p.sco, p.wdo_d, D),
                            (p.qwk, p.sck, p.wdk_d, KV), (p.qwv, p.scv, p.wdv_d, KV)):
        for g in range(32):
            for u in range(N // 512):
                p.dq_units.append((qw, sc, outd, g, u))
    p.dq_iss = 0
    p.dq_cmp = 0
    p.dq_busy = [None] * 8
    p.dq_stb = [None] * 8
    p.dq_ld = [None] * 8
    p.dq_bc = [None] * 8
    p.dq_dep0 = kb.snap()


def deq_step(kb, p, n):
    while p.dq_cmp < p.dq_iss:
        i = p.dq_cmp
        qw, sc, outd, g, u = p.dq_units[i]
        sl = i % 8
        ci = 8192 + sl * 512
        cs = 12288 + sl * 512
        t2 = kb.op("vector", lambda e, r, ci=ci, cs=cs, sl=sl: e.scalar_tensor_tensor(
            p.lntok[:, sl * 512:sl * 512 + 512], p.W2[:, ci:ci + 512], -8.0,
            p.W2[:, cs:cs + 512], ALU.add, ALU.mult),
            deps=[p.dq_bc[sl], p.dq_stb[sl]], inc="dve")
        st = dma(kb, "sync", outd.ap()[g * 128:(g + 1) * 128, u * 512:(u + 1) * 512],
                 p.lntok[:, sl * 512:sl * 512 + 512], deps=[t2], slot=f"dqs{sl}")
        p.dq_busy[sl] = t2
        p.dq_stb[sl] = st
        p.dq_cmp += 1
    for _ in range(n):
        if p.dq_iss >= len(p.dq_units):
            return
        i = p.dq_iss
        qw, sc, outd, g, u = p.dq_units[i]
        sl = i % 8
        dep0 = p.dq_dep0 if i < 8 else []
        p.dq_ld[sl] = dma(kb, "gpsimd", p.W2[:, 8192 + sl * 512:8192 + sl * 512 + 512],
                          qw.ap()[g * 128:(g + 1) * 128, u * 512:(u + 1) * 512],
                          deps=[p.dq_busy[sl]] + dep0, slot=f"dqi{sl}")
        p.dq_bc[sl] = dma(kb, "gpsimd", p.W2[:, 12288 + sl * 512:12288 + sl * 512 + 512],
                          sc.ap()[g:g + 1, u * 512:(u + 1) * 512].partition_broadcast(128),
                          deps=[p.dq_busy[sl]] + dep0, slot=f"dqi{sl}")
        p.dq_iss += 1


def deq_finish(kb, p):
    if not hasattr(p, "dq_iss"):
        deq_prep(kb, p)
    while p.dq_cmp < len(p.dq_units):
        deq_step(kb, p, len(p.dq_units) - p.dq_iss)
    p.deq_done = [t for t in p.dq_busy + p.dq_stb if t is not None]
    for nm, d in (("d_wdq", p.wdq_d), ("d_wdo", p.wdo_d)):
        if nm in p.dbg_t:
            dump_dram(kb, p, p.dbg_t[nm], d, p.deq_done)


def attn_b_proj(kb, p, ln_done):
    W2f = p.W2.ap().bitcast(F32)
    rope_ld = []
    for i, nm in enumerate(("ropekc", "ropeks", "ropeqc", "ropeqs")):
        rope_ld.append(dma(kb, "sync", W2f[:, i * 1024:(i + 1) * 1024], getattr(p, nm).ap(),
                           deps=list(p.setup) + p.deq_done + list(ln_done), slot="brl"))

    def rope_pair2(mm_dep, bank, tc, is_q, out_view, outdep=()):
        cv = kb.op("vector", lambda e, r, bank=bank: e.tensor_copy(p.evs[:, 0:512], p.psF[bank][:, :]),
                   deps=[mm_dep, p.g("rp_t1"), p.g("rp_cb")], inc="dve")
        cb = kb.op("vector", lambda e, r, bank=bank: e.tensor_copy(p.pm_sb[:, 1536:2048],
                                                                   p.psF[bank][:, :]),
                   deps=[mm_dep, p.g("rp_rot")], inc="dve")
        p.rp_cb = cb
        rot = kb.op("tensor", lambda e, r: e.matmul(p.psF[5][:, :], p.c_rot[:, :],
                                                    p.pm_sb[:, 1536:2048], start=True, stop=True),
                    deps=[cb, p.g("rp_t2")], inc="pe")
        p.rp_rot = rot
        base = 2048 if is_q else 0
        t1 = kb.op("vector", lambda e, r, tc=tc, base=base: e.tensor_tensor(
            p.evs2[:, 0:512], p.evs[:, 0:512], W2f[:, base + tc * 512:base + (tc + 1) * 512], ALU.mult),
            deps=[cv] + rope_ld, inc="dve")
        p.rp_t1 = t1
        t2 = kb.op("vector", lambda e, r, tc=tc, base=base: e.tensor_tensor(
            p.xt[:, 0:512], p.psF[5][:, :],
            W2f[:, base + 1024 + tc * 512:base + 1024 + (tc + 1) * 512], ALU.mult),
            deps=[rot, t1], inc="dve")
        p.rp_t2 = t2
        fin = kb.op("vector", lambda e, r, out_view=out_view: e.tensor_tensor(
            out_view, p.evs2[:, 0:512], p.xt[:, 0:512], ALU.add),
            deps=[t2] + list(outdep), inc="dve")
        return fin

    wcons = [None, None]
    kst = None
    vst = None
    for kind in ("k", "v"):
        wsrc = p.wdk_d if kind == "k" else p.wdv_d
        for ot in range(8):
            sl = ot % 2
            wld = dma(kb, "gpsimd", s3(p.W1[:, sl * 4096:sl * 4096 + 32 * 128], 32),
                      r3(wsrc.ap()[:, ot * 128:(ot + 1) * 128]),
                      deps=ln_done + p.deq_done + [wcons[sl]], slot=f"bw{sl}")
            for tc in range(2):
                bank = tc
                mm = None
                for dm in range(32):
                    mm = kb.op("tensor", lambda e, r, dm=dm, tc=tc, sl=sl, bank=bank: e.matmul(
                        p.psF[bank][:, :], p.W1[:, sl * 4096 + dm * 128:sl * 4096 + (dm + 1) * 128],
                        p.A1[:, dm * 1024 + tc * 512:dm * 1024 + (tc + 1) * 512],
                        start=(dm == 0), stop=(dm == 31)),
                        deps=([wld, p.g(f"bpev_{bank}")] if dm == 0 else []),
                        inc=("pe" if dm == 31 else None))
                if kind == "k":
                    fin = rope_pair2(mm, bank, tc, False, p.p_sb[:, 0:512], outdep=[kst])
                    p.__dict__[f"bpev_{bank}"] = fin
                    kst = dma(kb, "sync", p.kb_in.ap()[ot * 128:(ot + 1) * 128, tc * 512:(tc + 1) * 512],
                              p.p_sb[:, 0:512], deps=[fin], slot="bkst")
                    sqk = kb.op("scalar", lambda e, r: e.activation(p.pm_sb[:, 0:512], p.p_sb[:, 0:512],
                                                                    AF.Square), deps=[fin, p.g("bkn_mm")],
                               inc="act")
                    nm4 = None
                    for qt in range(4):
                        nm4 = kb.op("tensor", lambda e, r, qt=qt: e.matmul(
                            p.psF[4][:, qt:qt + 1], p.pm_sb[:, qt * 128:(qt + 1) * 128], p.c_onesc[:, :],
                            start=True, stop=True), deps=[sqk, p.g("bkn_cp")],
                            inc=("pe" if qt == 3 else None))
                    p.bkn_mm = nm4
                    cpn = kb.op("vector", lambda e, r: e.tensor_copy(p.lstg[:, 4:8], p.psF[4][:, 0:4]),
                                deps=[nm4], inc="dve")
                    p.bkn_cp = cpn
                    mr = kb.op("vector", lambda e, r: e.tensor_reduce(p.lstg[:, 8:9],
                                                                      p.lstg[:, 4:8], AX.X, ALU.max),
                               deps=[cpn], inc="dve")
                    km = kb.op("vector", lambda e, r, ot=ot, tc=tc: (
                        e.tensor_copy(p.kmx[:, 8 + ot:9 + ot], p.lstg[:, 8:9]) if tc == 0 else
                        e.tensor_tensor(p.kmx[:, 8 + ot:9 + ot], p.kmx[:, 8 + ot:9 + ot],
                                        p.lstg[:, 8:9], ALU.max)), deps=[mr], inc="dve")
                else:
                    cv = kb.op("scalar", lambda e, r, bank=bank: e.activation(
                        p.p_sb[:, 0:512], p.psF[bank][:, :], AF.Copy),
                        deps=[mm, p.g("bvtr"), kst], inc="act")
                    p.__dict__[f"bpev_{bank}"] = cv
                    trs = None
                    for k in range(4):
                        trs = kb.op("tensor", lambda e, r, k=k: e.transpose(
                            p.psB[1][:, k * 128:(k + 1) * 128], p.p_sb[:, k * 128:(k + 1) * 128],
                            p.c_ident[:, :]), deps=[cv, p.g("bvcp")], inc=("pe" if k == 3 else None))
                    p.bvtr = trs
                    cpv = kb.op("vector", lambda e, r: e.tensor_copy(p.pt_sb[:, 0:512], p.psB[1][:, 0:512]),
                                deps=[trs, vst], inc="dve")
                    p.bvcp = cpv
                    vst = dma(kb, "gpsimd",
                              r3(p.vb_in.ap()[tc * 512:(tc + 1) * 512, ot * 128:(ot + 1) * 128]),
                              s3(p.pt_sb[:, 0:512], 4), deps=[cpv], slot="bvst")
            wcons[sl] = kb.tok("pe")
        if kind == "k":
            kag = kb.op("gpsimd", lambda e, r: e.collective_compute(
                "AllGather", ALU.bypass, replica_groups=[list(range(NC))],
                ins=[p.kb_in.ap()], outs=[p.kb_ag.ap()]), deps=[kst], inc="cc")
    vag = kb.op("gpsimd", lambda e, r: e.collective_compute(
        "AllGather", ALU.bypass, replica_groups=[list(range(NC))],
        ins=[p.vb_in.ap()], outs=[p.vb_ag.ap()]), deps=[vst], inc="cc")

    kb.fence(("tensor", "vector", "scalar"))
    tr = kb.op("tensor", lambda e, r: e.transpose(p.psF[4][0:8, 0:128], p.kmx[:, 8:16], p.c_identf[:, :]),
               deps=[], inc="pe")
    cp = kb.op("vector", lambda e, r: e.tensor_copy(p.evs[0:8, 0:128], p.psF[4][0:8, 0:128]),
               deps=[tr], inc="dve")
    mx = kb.op("vector", lambda e, r: e.tensor_reduce(p.evs[0:8, 128:129], p.evs[0:8, 0:128], AX.X,
                                                      ALU.max), deps=[cp], inc="dve")
    tr2 = kb.op("tensor", lambda e, r: e.transpose(p.psF[4][0:1, 128:136], p.evs[0:8, 128:129],
                                                   p.c_identf[0:8, 0:8]), deps=[mx], inc="pe")
    cp2 = kb.op("vector", lambda e, r: e.tensor_copy(p.row[0:1, 0:8], p.psF[4][0:1, 128:136]),
                deps=[tr2], inc="dve")
    stk = dma(kb, "gpsimd", p.km_in.ap(), p.row[0:1, 0:8], deps=[cp2], slot="kms")

    kmr = kb.op("gpsimd", lambda e, r: e.collective_compute(
        "AllReduce", ALU.max, replica_groups=[list(range(NC))],
        ins=[p.km_in.ap()], outs=[p.km_out.ap()]), deps=[stk], inc="cc")
    p.bkv_ag = [kag, vag, kmr]
    kldm = dma(kb, "sync", p.row[0:1, 8:16], p.km_out.ap(), deps=[kmr], slot="kml")
    ksq = kb.op("scalar", lambda e, r: e.activation(p.row[0:1, 16:24], p.row[0:1, 8:16],
                                                    AF.Sqrt), deps=[kldm], inc="act")
    kng = kb.op("vector", lambda e, r: e.tensor_scalar(p.row[0:1, 24:32], p.row[0:1, 16:24],
                                                       -1.0, None, ALU.mult),
                deps=[ksq], inc="dve")
    p.bkmneg = kng
    p.bqn_sr = cp2

    wcons = [None, None]
    qst = None
    for hi in range(32):
        sl = hi % 2
        wld = dma(kb, "gpsimd", s3(p.W1[:, sl * 4096:sl * 4096 + 32 * 128], 32),
                  r3(p.wdq_d.ap()[:, hi * 128:(hi + 1) * 128]),
                  deps=ln_done + p.deq_done + [wcons[sl]], slot=f"bqw{sl}")
        for tc in range(2):
            bank = tc
            mm = None
            for dm in range(32):
                mm = kb.op("tensor", lambda e, r, dm=dm, tc=tc, sl=sl, bank=bank: e.matmul(
                    p.psF[bank][:, :], p.W1[:, sl * 4096 + dm * 128:sl * 4096 + (dm + 1) * 128],
                    p.A1[:, dm * 1024 + tc * 512:dm * 1024 + (tc + 1) * 512],
                    start=(dm == 0), stop=(dm == 31)),
                    deps=([wld, p.g(f"bpev_{bank}")] if dm == 0 else []),
                    inc=("pe" if dm == 31 else None))
            fin = rope_pair2(mm, bank, tc, True, p.p_sb[:, 0:512],
                             outdep=[qst, p.g("bvtr")])
            p.__dict__[f"bpev_{bank}"] = fin
            qst = dma(kb, "sync",
                      r3(p.qb_d.ap()[hi * 1024 + tc * 512:hi * 1024 + (tc + 1) * 512, 0:128]),
                      s3(p.p_sb[:, 0:512], 4), deps=[fin], slot="bqst")
            sqq = kb.op("scalar", lambda e, r: e.activation(p.pm_sb[:, 0:512], p.p_sb[:, 0:512],
                                                            AF.Square), deps=[fin, p.g("bqn_mm")],
                        inc="act")
            rmm = kb.op("tensor", lambda e, r: e.matmul(
                p.psF[4][0:1, 0:512], p.c_onesc[:, :], p.pm_sb[:, 0:512],
                start=True, stop=True), deps=[sqq, p.g("bqn_sr")], inc="pe")
            p.bqn_mm = rmm
            srr = kb.op("scalar", lambda e, r: e.activation(
                p.rowb[0:1, 0:512], p.psF[4][0:1, 0:512], AF.Sqrt),
                deps=[rmm, p.g("bbst")], inc="act")
            p.bqn_sr = srr
            fx = kb.op("vector", lambda e, r, hi=hi: e.tensor_scalar(
                p.rowb[0:1, 0:512], p.rowb[0:1, 0:512],
                p.row[0:1, 24 + hi // 4:25 + hi // 4], None, ALU.mult),
                deps=[srr, p.bkmneg], inc="dve")
            bst = dma(kb, "sync", p.browd.ap()[hi * 2 + tc:hi * 2 + tc + 1, :],
                      p.rowb[0:1, 0:512], deps=[fx], slot="bbst")
            p.bbst = bst
        wcons[sl] = kb.tok("pe")
    p.bproj_done = kb.snap() + p.bkv_ag
    for nm, d in (("d_qb", p.qb_d), ("d_kag", p.kb_ag), ("d_vag", p.vb_ag)):
        if nm in p.dbg_t:
            dump_dram(kb, p, p.dbg_t[nm], d, p.bproj_done)
    if "d_bound" in p.dbg_t:
        cpd = kb.op("vector", lambda e, r: e.tensor_copy(p.evs[:, 0:256], p.bnd[:, :]),
                    deps=p.bproj_done, inc="dve")
        dma(kb, "gpsimd", p.dbg_t["d_bound"].ap(), p.evs[:, 0:256], deps=[cpd], slot="dbg")
    if "d_kmax" in p.dbg_t:
        cpk = kb.op("vector", lambda e, r: e.tensor_copy(p.evs[0:1, 0:8], p.row[0:1, 8:16]),
                    deps=p.bproj_done, inc="dve")
        dma(kb, "gpsimd", p.dbg_t["d_kmax"].ap(), p.evs[0:1, 0:8], deps=[cpk], slot="dbg")


def attn_b_attend(kb, p):
    kb.fence()
    W2f = p.W2.ap().bitcast(F32)
    OACC = 4096
    z1 = kb.op("vector", lambda e, r: e.memset(W2f[:, OACC:OACC + 4096], 0.0),
               deps=p.bproj_done, inc="dve")
    p.boad = z1

    exp_busy = [None] * 3
    mst_busy = None
    sb_busy = [None, None]
    qcons = None
    lcons = None
    brcons = None
    for g in range(8):
        qldg = dma(kb, "sync", s3(p.W1[:, 0:4096], 32),
                   r3(p.qb_d.ap()[4 * g * 1024:(4 * g + 4) * 1024, :]),
                   deps=(p.bproj_done if g == 0 else []) + [qcons], slot="bq")
        for qsel in (1, 0):
            brld = []
            for h in range(4):
                brld.append(dma(kb, "sync", p.brst[h][0:1, :],
                                p.browd.ap()[(4 * g + h) * 2 + qsel:(4 * g + h) * 2 + qsel + 1, :],
                                deps=[brcons] + (p.bproj_done if g == 0 and qsel == 1 else []),
                                slot=f"bbr{h}"))
            nkc = 16 if qsel == 1 else 8
            for kc in range(nkc):
                sl = kc % 2
                r_, colb = chunk_owner(kc)
                dep0 = p.bproj_done if (g == 0 and qsel == 1 and kc < 2) else []
                kld = dma(kb, "sync", p.W2[:, 4096 + sl * 1024:4096 + sl * 1024 + 512],
                          p.kb_ag.ap()[1024 * r_ + 128 * g:1024 * r_ + 128 * (g + 1),
                                       colb:colb + 512], deps=[sb_busy[sl]] + dep0, slot=f"bak{sl}")
                vld = dma(kb, "sync", s3(p.W2[:, 4096 + sl * 1024 + 512:4096 + sl * 1024 + 1024], 4),
                          r3(p.vb_ag.ap()[1024 * r_ + colb:1024 * r_ + colb + 512,
                                          128 * g:128 * (g + 1)]), deps=[sb_busy[sl]] + dep0,
                          slot=f"bav{sl}")
                masked = (qsel == 0) or (kc >= 8)
                if masked:
                    mld = dma(kb, "sync", p.W2[:, sl * 2048:(sl + 1) * 2048],
                              p.bmasks.ap()[kc * 128:(kc + 1) * 128, :],
                              deps=[sb_busy[sl]] + dep0, slot=f"bam{sl}")
                for h in range(4):
                    psLb = p.psF[3] if h < 2 else p.psF[5]
                    lrow = (h % 2) * 32
                    exs = [None] * 4
                    msts = [None] * 4
                    for kt in range(4):
                        bank = kt % 3
                        kb.op("tensor", lambda e, r, bank=bank, h=h: e.matmul(
                            p.psF[bank][:, :], p.c_onesr[0:1, 0:128],
                            p.brst[h][0:1, :], start=True, stop=False),
                            deps=[exp_busy[bank], brld[h]], inc=None)
                        smm = kb.op("tensor", lambda e, r, bank=bank, h=h, kt=kt, sl=sl, qsel=qsel: e.matmul(
                            p.psF[bank][:, :],
                            p.W2[:, 4096 + sl * 1024 + kt * 128:4096 + sl * 1024 + (kt + 1) * 128],
                            p.W1[:, h * 1024 + qsel * 512:h * 1024 + (qsel + 1) * 512],
                            start=False, stop=True), deps=[kld, qldg], inc="pe")
                        ex = kb.op("scalar", lambda e, r, bank=bank, kt=kt: e.activation(
                            p.p_sb[:, kt * 512:(kt + 1) * 512], p.psF[bank][:, :], AF.Exp),
                            deps=[smm, mst_busy], inc="act")
                        exp_busy[bank] = ex
                        exs[kt] = ex
                        if masked:
                            mst = kb.op("vector", lambda e, r, kt=kt, sl=sl: e.tensor_tensor(
                                p.pm_sb[:, kt * 512:(kt + 1) * 512], p.p_sb[:, kt * 512:(kt + 1) * 512],
                                p.W2[:, sl * 2048 + kt * 512:sl * 2048 + (kt + 1) * 512], ALU.mult),
                                deps=[ex, mld, p.g("bmst2")], inc="dve")
                            p.bmst2 = mst
                            msts[kt] = mst
                    if masked:
                        mst_busy = p.bmst2
                    psrc = p.pm_sb if masked else p.p_sb
                    for kt in range(4):
                        kb.op("tensor", lambda e, r, kt=kt, kc=kc, nkc=nkc, psrc=psrc, psLb=psLb,
                              lrow=lrow: e.matmul(
                            psLb[lrow:lrow + 1, 0:512], p.c_onesc[:, :],
                            psrc[:, kt * 512:(kt + 1) * 512],
                            start=(kc == 0 and kt == 0), stop=(kc == nkc - 1 and kt == 3),
                            skip_group_check=True),
                            deps=[msts[kt] or exs[kt]] + ([lcons] if kc == 0 and kt == 0 else []),
                            inc=None)
                    pvl = None
                    for kt in range(4):
                        pvl = kb.op("tensor", lambda e, r, kt=kt, sl=sl, psrc=psrc: e.matmul(
                            p.psF[4][:, :],
                            p.W2[:, 4096 + sl * 1024 + 512 + kt * 128:4096 + sl * 1024 + 512 + (kt + 1) * 128],
                            psrc[:, kt * 512:(kt + 1) * 512], start=(kt == 0), stop=(kt == 3)),
                            deps=[msts[kt] or exs[kt], vld] + ([p.boad] if kt == 0 else []),
                            inc=("pe" if kt == 3 else None))
                    oad = kb.op("vector", lambda e, r, h=h, qsel=qsel: e.tensor_tensor(
                        W2f[:, OACC + h * 1024 + qsel * 512:OACC + h * 1024 + qsel * 512 + 512],
                        W2f[:, OACC + h * 1024 + qsel * 512:OACC + h * 1024 + qsel * 512 + 512],
                        p.psF[4][:, :], ALU.add), deps=[pvl, p.boad], inc="dve")
                    p.boad = oad
                sb_busy[sl] = kb.tok("pe")
            brcons = kb.tok("pe")
            getok = kb.tok("pe")
            ge_busy = None
            for h in range(4):
                psLb = p.psF[3] if h < 2 else p.psF[5]
                lrow = (h % 2) * 32
                rc = kb.op("vector", lambda e, r, psLb=psLb, lrow=lrow: e.reciprocal(
                    p.evs[0:1, 0:512], psLb[lrow:lrow + 1, 0:512]),
                    deps=[getok, ge_busy], inc="dve")
                bcm = kb.op("tensor", lambda e, r: e.matmul(
                    p.psF[4][:, :], p.c_onesrf[0:1, 0:128], p.evs[0:1, 0:512],
                    start=True, stop=True), deps=[rc, p.boad, p.g("bctxop")], inc="pe")
                ctx = kb.op("vector", lambda e, r, g=g, h=h, qsel=qsel: e.tensor_tensor(
                    p.A1[:, (4 * g + h) * 1024 + qsel * 512:(4 * g + h) * 1024 + (qsel + 1) * 512],
                    W2f[:, OACC + h * 1024 + qsel * 512:OACC + h * 1024 + (qsel + 1) * 512],
                    p.psF[4][:, :], ALU.mult), deps=[bcm], inc="dve")
                p.bctxop = ctx
                rs1 = kb.op("vector", lambda e, r, h=h, qsel=qsel: e.memset(
                    W2f[:, OACC + h * 1024 + qsel * 512:OACC + h * 1024 + (qsel + 1) * 512], 0.0),
                    deps=[ctx], inc="dve")
                ge_busy = rs1
                lcons = rc
            p.boad = ge_busy
        qcons = kb.tok("pe")
    p.batt_done = kb.snap()
    if "d_ctx" in p.dbg_t:
        bus = None
        for i in range(64):
            cpc = kb.op("vector", lambda e, r, i=i: e.tensor_copy(p.evs[:, 0:512],
                                                                  p.A1[:, i * 512:(i + 1) * 512]),
                        deps=p.batt_done + [bus], inc="dve")
            bus = dma(kb, "gpsimd", p.dbg_t["d_ctx"].ap()[:, i * 512:(i + 1) * 512], p.evs[:, 0:512],
                      deps=[cpc], slot="dbg")


def attn_b_oproj(kb, p):
    kb.fence()
    wcons = [None, None]
    ev_busy = [None, None]
    xst = {}
    for oc in range(8):
        half = oc % 2
        W = p.W1 if half == 0 else p.W2
        wld = dma(kb, "gpsimd", s3(W[:, 0:32 * 512], 32),
                  r3(p.wdo_d.ap()[:, oc * 512:(oc + 1) * 512]),
                  deps=p.batt_done + [wcons[half]], slot=f"ow{half}")
        for tt in range(8):
            bank = tt % 2
            mm = None
            for dm in range(32):
                mm = kb.op("tensor", lambda e, r, dm=dm, tt=tt, W=W, bank=bank: e.matmul(
                    p.psF[bank][:, :], p.A1[:, dm * 1024 + tt * 128:dm * 1024 + (tt + 1) * 128],
                    W[:, dm * 512:(dm + 1) * 512], start=(dm == 0), stop=(dm == 31)),
                    deps=([wld, ev_busy[bank]] if dm == 0 else []), inc=("pe" if dm == 31 else None))
            co = bank * 512
            xld = dma(kb, "sync", p.evs[:, co:co + 512],
                      p.xw.ap()[tt * 128:(tt + 1) * 128, oc * 512:(oc + 1) * 512],
                      deps=[ev_busy[bank]] + (p.batt_done if oc < 2 else []), slot=f"ox{bank}")
            ev = kb.op("vector", lambda e, r, bank=bank, co=co: e.tensor_tensor(
                p.evs2[:, co:co + 512], p.psF[bank][:, :], p.evs[:, co:co + 512], ALU.add),
                deps=[mm, xld, ev_busy[bank], xst.get(bank)], inc="dve")
            ev_busy[bank] = ev
            st = dma(kb, "sync", p.xw.ap()[tt * 128:(tt + 1) * 128, oc * 512:(oc + 1) * 512],
                     p.evs2[:, co:co + 512], deps=[ev], slot=f"oxs{bank}")
            xst[bank] = st
        wcons[half] = kb.tok("pe")
    p.sub3_done = kb.snap()




def dump_dram(kb, p, dst, src, deps):
    rows, cols = src.shape
    bus = None
    for r0 in range(0, rows, 128):
        for c0 in range(0, cols, 512):
            cc = min(512, cols - c0)
            ld = dma(kb, "gpsimd", p.evs[:, 0:cc], src.ap()[r0:r0 + 128, c0:c0 + cc],
                     deps=list(deps) + [bus], slot="dbg")
            bus = dma(kb, "gpsimd", dst.ap()[r0:r0 + 128, c0:c0 + cc], p.evs[:, 0:cc], deps=[ld],
                      slot="dbg")



_CACHE = {}


def prep_inputs(ins):
    st = stored_tokens()
    x_full = np.asarray(ins["L0_attn_a_hidden_state"])[0]
    wq = np.asarray(ins["L0_attn_a_wq"]); bq = np.asarray(ins["L0_attn_a_bq"])
    wk = np.asarray(ins["L0_attn_a_wk"]); bk = np.asarray(ins["L0_attn_a_bk"])
    wv = np.asarray(ins["L0_attn_a_wv"]); bv = np.asarray(ins["L0_attn_a_bv"])
    wo = np.asarray(ins["L0_attn_a_wo"]); bo = np.asarray(ins["L0_attn_a_bo"])
    A = np.asarray(ins["L0_ffn_A"]); B = np.asarray(ins["L0_ffn_B"])
    r4 = np.arange(4096)
    sig4 = 256 * (r4 // 256) + 2 * (r4 % 128) + (r4 % 256) // 128
    base = dict(
        wo_a=np.ascontiguousarray(wo[sig4]), bo_a=np.ascontiguousarray(bo[None, :]),
        wbT=np.ascontiguousarray(np.asarray(ins["L0_ffn_W_base"]).T),
        bb=np.ascontiguousarray(np.asarray(ins["L0_ffn_b_base"])[None, :]),
        wrT=np.ascontiguousarray(np.asarray(ins["L0_ffn_W_router"]).T),
        acT=np.ascontiguousarray(A.transpose(2, 0, 1).reshape(D, ER)),
        bcat=np.ascontiguousarray(B.transpose(0, 2, 1).reshape(ER, D)),
        qwq=np.ascontiguousarray(np.asarray(ins["L0_attn_b_qw_q"])),
        qwk=np.ascontiguousarray(np.asarray(ins["L0_attn_b_qw_k"])),
        qwv=np.ascontiguousarray(np.asarray(ins["L0_attn_b_qw_v"])),
        qwo=np.ascontiguousarray(np.asarray(ins["L0_attn_b_qw_o"])),
        scq=np.ascontiguousarray(np.asarray(ins["L0_attn_b_scales_q"])),
        sck=np.ascontiguousarray(np.asarray(ins["L0_attn_b_scales_k"])),
        scv=np.ascontiguousarray(np.asarray(ins["L0_attn_b_scales_v"])),
        sco=np.ascontiguousarray(np.asarray(ins["L0_attn_b_scales_o"])),
        ident=np.eye(128, dtype=ml_dtypes.bfloat16),
        identf=np.eye(128, dtype=np.float32),
        onesr=np.ones((1, 512), ml_dtypes.bfloat16),
        onesc=np.ones((128, 1), ml_dtypes.bfloat16),
        onesrf=np.ones((1, 128), np.float32),
        emat=np.ascontiguousarray(np.eye(8, dtype=np.float32).repeat(16, axis=1)).astype(ml_dtypes.bfloat16),
        rotm=host_rot().astype(ml_dtypes.bfloat16),
    )
    in_maps = []
    for c in range(NC):
        toks = st[core_stored_rows(c)]
        qcols = np.concatenate([np.arange(h * HD, (h + 1) * HD) for h in CORE_HEADS[c]])
        kcols = np.concatenate([np.arange(g * HD, (g + 1) * HD) for g in CORE_GROUPS[c]])
        cqc, cqs, ckc, cks = host_rope(toks)
        m = dict(base)
        m.update(x=np.ascontiguousarray(x_full[toks]),
                 wq_a=np.ascontiguousarray(wq[:, qcols]),
                 wk_a=np.ascontiguousarray(wk[:, kcols]),
                 wv_a=np.ascontiguousarray(wv[:, kcols]),
                 bq_a=np.ascontiguousarray(bq[qcols].reshape(4, 128).T),
                 bk_a=np.ascontiguousarray(bk[kcols].reshape(2, 128).T),
                 bv_a=np.ascontiguousarray(bv[kcols].reshape(2, 128).T),
                 ropeqc=cqc, ropeqs=cqs, ropekc=ckc, ropeks=cks,
                 bmasks=build_bmasks(c))
        in_maps.append(m)
    return in_maps, st


def kernel(**ins):
    key = (tuple(sorted(DBG)), PHASES, SUB1, ATT_H, ATT_QC, SKIP)
    if key not in _CACHE:
        _CACHE[key] = build_program(DBG)
    nc = _CACHE[key]
    in_maps, st = prep_inputs(ins)
    res = run_bass_kernel_spmd(nc, in_maps, core_ids=list(range(NC)))
    out = np.zeros((1, S, D), np.float32)
    for c in range(NC):
        out[0, st[core_stored_rows(c)], :] = np.asarray(res.results[c]["out"], np.float32)
    kernel.last = res.results
    return out


# version 06 (final)
# speedup vs baseline: 1.2801x; 1.2801x over previous
import numpy as np
import ml_dtypes
import concourse.bass as bass
import concourse.mybir as mybir
from concourse.bass_utils import run_bass_kernel_spmd
from contextlib import ExitStack

F32 = mybir.dt.float32
BF16 = mybir.dt.bfloat16
I32 = mybir.dt.int32
AF = mybir.ActivationFunctionType
ALU = mybir.AluOpType
AX = mybir.AxisListType

S, D, NH, HD, NKV = 8192, 4096, 32, 128, 8
KV = NKV * HD
NC, TPC = 8, 1024
ISCALE = 0.08838834764831845
NEXP, RANK = 8, 16
ER = NEXP * RANK
LN_EPS = 1e-5

DBG = ()
PHASES = 4
SUB1 = 3
ATT_H = 4
ATT_QC = 16
SKIP = 0



def stored_tokens():
    s = np.arange(S)
    return 256 * (s // 256) + 2 * (s % 128) + (s % 256) // 128


def core_stored_rows(c):
    return np.concatenate([np.arange(512 * c, 512 * c + 512),
                           np.arange(512 * (15 - c), 512 * (16 - c))])


CORE_HEADS = [[2 * c, 2 * c + 1, 30 - 2 * c, 31 - 2 * c] for c in range(NC)]
CORE_GROUPS = [[c // 2, (15 - c) // 2] for c in range(NC)]


def superblocks(c):
    return [(0, kc) for kc in range(c + 1)] + [(1, kc) for kc in range(16 - c)]


def chunk_owner(kc):
    return (kc, 0) if kc < 8 else (15 - kc, 512)


def build_bmasks(c):
    incl = (np.arange(128)[None, :] <= np.arange(128)[:, None]).astype(np.float32)
    strict = (np.arange(128)[None, :] <= (np.arange(128)[:, None] - 1)).astype(np.float32)
    diag = np.zeros((4, 128, 512), np.float32)
    for qt in range(4):
        for kt in range(4):
            if kt // 2 < qt // 2:
                blk = np.ones((128, 128), np.float32)
            elif kt // 2 > qt // 2:
                blk = np.zeros((128, 128), np.float32)
            else:
                blk = strict if (qt % 2 == 0 and kt % 2 == 1) else incl
            diag[qt, :, kt * 128:(kt + 1) * 128] = blk
    diagT = np.zeros((128, 2048), np.float32)
    for kt in range(4):
        for qt in range(4):
            diagT[:, kt * 512 + qt * 128:kt * 512 + (qt + 1) * 128] = \
                diag[qt][:, kt * 128:(kt + 1) * 128].T
    out = np.zeros((16, 128, 2048), np.float32)
    for kc in range(16):
        chunk = c if kc < 8 else 15 - c
        if kc < chunk:
            out[kc] = 1.0
        elif kc == chunk:
            out[kc] = diagT
    return np.ascontiguousarray(out.reshape(16 * 128, 2048)).astype(ml_dtypes.bfloat16)


def host_rope(tok):
    inv = (1.0 / (10000.0 ** (np.arange(0, HD, 2, dtype=np.float32) / HD))).astype(np.float32)
    fr = tok.astype(np.float32)[:, None] * inv[None, :]
    emb = np.concatenate([fr, fr], axis=1).astype(np.float32)
    cos, sin = np.cos(emb).astype(np.float32).T, np.sin(emb).astype(np.float32).T
    return (np.ascontiguousarray(cos * np.float32(ISCALE)),
            np.ascontiguousarray(sin * np.float32(ISCALE)),
            np.ascontiguousarray(cos), np.ascontiguousarray(sin))


def host_rot():
    M = np.zeros((128, 128), np.float32)
    for d in range(64):
        M[d + 64, d] = -1.0
        M[d, d + 64] = 1.0
    return M



ENGS = ("tensor", "scalar", "vector", "gpsimd", "sync")
SEMS = ("pe", "act", "dve", "pool", "cc", "st")
NSLOTS = 88


class KB:
    def __init__(self, nc):
        self.nc = nc
        self.stream = {e: [] for e in ENGS}
        self.cnt = {s: 0 for s in SEMS}
        self.sems = {}
        self.slotmap = {}
        self.waited = {e: {} for e in ENGS}
        self.regnames = {e: [] for e in ENGS}
        self.regs = {e: {} for e in ENGS}

    def tok(self, name):
        return (name, self.cnt[name])

    def slot(self, name):
        if name not in self.slotmap:
            i = len(self.slotmap)
            assert i < NSLOTS, "out of dma slots"
            self.slotmap[name] = f"q{i}"
            self.cnt.setdefault(f"q{i}", 0)
        return self.slotmap[name]

    def snap(self):
        return [self.tok(s) for s in list(self.cnt)]

    def op(self, eng, build, deps=(), inc=None, inck=1):
        w = self.waited[eng]
        need = {}
        for d in deps:
            if d is None:
                continue
            s, v = d
            if v > need.get(s, 0):
                need[s] = v
        use = []
        for s, v in need.items():
            if v <= 0 or w.get(s, 0) >= v:
                continue
            w[s] = v
            use.append((s, v))
        if inc is not None:
            self.cnt[inc] += inck

        def fn(e, kb=self, use=use, inc=inc, inck=inck, build=build):
            for (s, v) in use:
                e.wait_ge(kb.sems[s], v)
            ins = build(e, kb.regs)
            if inc is not None:
                ins.then_inc(kb.sems[inc], inck)
        self.stream[eng].append(fn)
        return (inc, self.cnt[inc]) if inc is not None else None

    def fence(self, engines=ENGS):
        sn = self.snap()
        for eng in engines:
            self.op(eng, lambda e, r: e.nop(), deps=sn)

    def reg(self, eng, name):
        if name not in self.regnames[eng]:
            self.regnames[eng].append(name)

    def emit(self, block):
        for eng in ENGS:
            ops = self.stream[eng]
            if not ops:
                continue

            def section(e, ops=ops, eng=eng):
                for nm in self.regnames[eng]:
                    self.regs[eng][nm] = e.alloc_register(nm)
                for f in ops:
                    f(e)
            getattr(block, eng)(section)


def dma(kb, q, dst, src, deps=(), slot="misc", sem=None):
    sem = sem or kb.slot(slot)
    return kb.op(q, lambda e, r: e.dma_start(out=dst, in_=src), deps=deps, inc=sem, inck=16)


class P:
    def g(self, key):
        return self.__dict__.get(key)


def r3(ap, p=128):
    return ap.rearrange("(a p) n -> p a n", p=p)


def s3(ap, a):
    return ap.rearrange("p (a n) -> p a n", a=a)




def build_program(dbg=(), phases=None):
    global PHASES
    if phases is not None:
        PHASES = phases
    nc = bass.Bass("TRN2", target_bir_lowering=False)
    kb = KB(nc)
    p = P()
    p.dbg = set(dbg)

    def din(name, shape, dt):
        setattr(p, name, nc.dram_tensor(name, list(shape), dt, kind="ExternalInput"))

    din("x", (TPC, D), F32)
    din("wq_a", (D, 512), F32); din("wk_a", (D, 256), F32); din("wv_a", (D, 256), F32)
    din("bq_a", (128, 4), F32); din("bk_a", (128, 2), F32); din("bv_a", (128, 2), F32)
    din("wo_a", (D, D), F32); din("bo_a", (1, D), F32)
    din("wbT", (D, D), F32); din("bb", (1, D), F32); din("wrT", (D, 8), F32)
    din("acT", (D, ER), F32); din("bcat", (ER, D), F32)
    din("qwq", (D, D), I32); din("qwk", (D, KV), I32); din("qwv", (D, KV), I32); din("qwo", (D, D), I32)
    din("scq", (32, D), F32); din("sck", (32, KV), F32); din("scv", (32, KV), F32); din("sco", (32, D), F32)
    din("ident", (128, 128), BF16); din("identf", (128, 128), F32); din("onesr", (1, 512), BF16); din("onesc", (128, 1), BF16)
    din("emat", (8, 128), BF16); din("rotm", (128, 128), BF16)
    din("ropeqc", (128, TPC), F32); din("ropeqs", (128, TPC), F32)
    din("ropekc", (128, TPC), F32); din("ropeks", (128, TPC), F32)
    din("bmasks", (16 * 128, 2048), BF16)
    din("onesrf", (1, 128), F32)

    p.out = nc.dram_tensor("out", [TPC, D], F32, kind="ExternalOutput")
    p.dbg_t = {}
    for name, shape in (("d_x1", (TPC, D)), ("d_x2", (TPC, D)), ("d_x3", (TPC, D)),
                        ("d_ln1ag", (NC * D, TPC)), ("d_qa", (512, S)), ("d_ka", (256, S)),
                        ("d_va", (S, 256)), ("d_oa", (4 * S, HD)), ("d_qb", (32 * TPC, HD)),
                        ("d_kag", (NC * KV, TPC)), ("d_vag", (NC * TPC, KV)),
                        ("d_wdq", (D, D)), ("d_wdo", (D, D)), ("d_ctx", (128, 32 * TPC)),
                        ("d_bound", (128, 256)), ("d_lsum", (128, 256)), ("d_kmax", (1, 8)),
                        ("d_hT", (128, 1024)), ("d_wexp", (128, 1024))):
        if name in p.dbg:
            p.dbg_t[name] = nc.dram_tensor(name, list(shape), F32, kind="ExternalOutput")

    p.xw = nc.dram_tensor("xw", [TPC, D], F32)
    p.ag1_in = nc.dram_tensor("ag1_in", [D, TPC], BF16)
    p.ag1_out = nc.dram_tensor("ag1_out", [NC * D, TPC], BF16)
    p.qa_d = nc.dram_tensor("qa_d", [512, S], BF16)
    p.ka_d = nc.dram_tensor("ka_d", [256, S], BF16)
    p.va_d = nc.dram_tensor("va_d", [S, 256], BF16)
    p.oa_d = nc.dram_tensor("oa_d", [4 * S, HD], BF16)
    p.qb_d = p.oa_d
    p.kb_in = nc.dram_tensor("kb_in", [KV, TPC], BF16)
    p.vb_in = nc.dram_tensor("vb_in", [TPC, KV], BF16)
    p.kb_ag = nc.dram_tensor("kb_ag", [NC * KV, TPC], BF16)
    p.vb_ag = nc.dram_tensor("vb_ag", [NC * TPC, KV], BF16)
    p.km_in = nc.dram_tensor("km_in", [1, 8], F32)
    p.km_out = nc.dram_tensor("km_out", [1, 8], F32)
    p.browd = nc.dram_tensor("browd", [64, 512], BF16)
    p.wdq_d = nc.dram_tensor("wdq_d", [D, D], BF16)
    p.wdk_d = nc.dram_tensor("wdk_d", [D, KV], BF16)
    p.wdv_d = nc.dram_tensor("wdv_d", [D, KV], BF16)
    p.wdo_d = nc.dram_tensor("wdo_d", [D, D], BF16)

    with ExitStack() as ctx:
        e = ctx.enter_context
        p.c_ident = e(nc.sbuf_tensor("c_ident", [128, 128], BF16))
        p.c_identf = e(nc.sbuf_tensor("c_identf", [128, 128], F32))
        p.c_onesr = e(nc.sbuf_tensor("c_onesr", [1, 512], BF16))
        p.c_onesc = e(nc.sbuf_tensor("c_onesc", [128, 1], BF16))
        p.c_onesrf = e(nc.sbuf_tensor("c_onesrf", [1, 128], F32))
        p.brst = [e(nc.sbuf_tensor(f"brst{i}", [1, 512], BF16)) for i in range(4)]
        p.c_emat = e(nc.sbuf_tensor("c_emat", [8, 128], BF16))
        p.c_rot = e(nc.sbuf_tensor("c_rot", [128, 128], BF16))
        p.c_bq = e(nc.sbuf_tensor("c_bq", [128, 4], F32))
        p.c_bk = e(nc.sbuf_tensor("c_bk", [128, 2], F32))
        p.c_bv = e(nc.sbuf_tensor("c_bv", [128, 2], F32))
        p.c_wrT = e(nc.sbuf_tensor("c_wrT", [128, 32 * 8], BF16))
        p.c_bcat = e(nc.sbuf_tensor("c_bcat", [128, D], BF16))
        p.A1 = e(nc.sbuf_tensor("A1", [128, 32768], BF16))
        p.W1 = e(nc.sbuf_tensor("W1", [128, 16384], BF16))
        p.W2 = e(nc.sbuf_tensor("W2", [128, 16384], BF16))
        p.xt = e(nc.sbuf_tensor("xt", [128, D], F32))
        p.lntok = e(nc.sbuf_tensor("lntok", [128, D], BF16))
        p.p_sb = e(nc.sbuf_tensor("p_sb", [128, 2048], BF16))
        p.pm_sb = e(nc.sbuf_tensor("pm_sb", [128, 2048], BF16))
        p.pt_sb = e(nc.sbuf_tensor("pt_sb", [128, 2048], BF16))
        p.stat = e(nc.sbuf_tensor("stat", [128, 16], F32))
        p.bnd = e(nc.sbuf_tensor("bnd", [128, 256], F32))
        p.lstg = e(nc.sbuf_tensor("lstg", [128, 16], F32))
        p.kmx = e(nc.sbuf_tensor("kmx", [128, 16], F32))
        p.row = e(nc.sbuf_tensor("row", [1, 1024], F32))
        p.rowb = e(nc.sbuf_tensor("rowb", [1, 512], BF16))
        p.brow = e(nc.sbuf_tensor("brow", [1, 1024], BF16))
        p.wexp = e(nc.sbuf_tensor("wexp", [128, 1024], BF16))
        p.hT = e(nc.sbuf_tensor("hT", [128, 1024], BF16))
        p.wrow = e(nc.sbuf_tensor("wrow", [8, 1024], BF16))
        p.evs = e(nc.sbuf_tensor("evs", [128, 1024], F32))
        p.evs2 = e(nc.sbuf_tensor("evs2", [128, 1024], F32))
        p.psF = [e(nc.psum_tensor(f"psF{i}", [128, 512], F32)) for i in range(6)]
        p.psB = [e(nc.psum_tensor(f"psB{i}", [128, 1024], BF16)) for i in range(2)]
        for s in SEMS:
            kb.sems[s] = e(nc.semaphore(s))
        for i in range(NSLOTS):
            kb.sems[f"q{i}"] = e(nc.semaphore(f"q{i}"))

        phase_setup(kb, p)
        phase_sub1(kb, p)
        if PHASES >= 2:
            phase_ffn(kb, p, 1)
        if PHASES >= 3:
            phase_sub3(kb, p)
        if PHASES >= 4:
            phase_ffn(kb, p, 3)
        if PHASES < 4:
            kb.fence()
            dma(kb, "gpsimd", p.out.ap(), p.xw.ap(), deps=kb.snap(), sem="st")
        fin = kb.snap()
        kb.op("gpsimd", lambda e, r: e.nop(), deps=fin)
        block = e(nc.Block())
        kb.emit(block)
    return nc


def phase_setup(kb, p):
    for dst, src in [(p.c_ident, p.ident), (p.c_identf, p.identf),
                     (p.c_onesr, p.onesr), (p.c_onesc, p.onesc),
                     (p.c_onesrf, p.onesrf),
                     (p.c_emat, p.emat), (p.c_rot, p.rotm),
                     (p.c_bq, p.bq_a), (p.c_bk, p.bk_a), (p.c_bv, p.bv_a)]:
        dma(kb, "sync", dst[:, :], src.ap(), slot="cst")
    dma(kb, "gpsimd", p.c_bcat[:, :], p.bcat.ap(), slot="cst2")
    dma(kb, "gpsimd", s3(p.c_wrT[:, :], 32), r3(p.wrT.ap()), slot="cst2")
    p.setup = [kb.tok(kb.slot("cst")), kb.tok(kb.slot("cst2"))]




def emit_ln(kb, p, src_dram, extra_deps=()):
    kb.fence()
    deps0 = list(p.setup) + list(extra_deps)
    xt_busy = None
    fscr_busy = None
    psb_busy = None
    for tt in range(8):
        ld = dma(kb, "sync", p.xt[:, :], src_dram.ap()[tt * 128:(tt + 1) * 128, :],
                 deps0 + [xt_busy], slot="xt")
        s1 = kb.op("vector", lambda e, r: e.reduce_sum(p.stat[:, 0:1], p.xt[:, :], axis=AX.X),
                   deps=[ld], inc="dve")
        W2sc = p.W2.ap().bitcast(F32)
        sq = kb.op("scalar", lambda e, r, W2sc=W2sc: e.activation(W2sc[:, 0:4096], p.xt[:, :], AF.Square),
                   deps=[ld, fscr_busy], inc="act")
        s2 = kb.op("vector", lambda e, r, W2sc=W2sc: e.reduce_sum(p.stat[:, 1:2], W2sc[:, 0:4096], axis=AX.X),
                   deps=[sq, s1], inc="dve")
        fscr_busy = s2
        mv = kb.op("vector", lambda e, r: e.tensor_scalar(p.stat[:, 2:4], p.stat[:, 0:2], 1.0 / D, None,
                                                          ALU.mult), deps=[s2], inc="dve")
        m2 = kb.op("vector", lambda e, r: e.tensor_tensor(p.stat[:, 4:5], p.stat[:, 2:3], p.stat[:, 2:3],
                                                          ALU.mult), deps=[mv], inc="dve")
        vv = kb.op("vector", lambda e, r: e.tensor_tensor(p.stat[:, 5:6], p.stat[:, 3:4], p.stat[:, 4:5],
                                                          ALU.subtract), deps=[m2], inc="dve")
        ve = kb.op("vector", lambda e, r: e.tensor_scalar(p.stat[:, 8:9], p.stat[:, 5:6],
                                                          float(LN_EPS), None, ALU.add),
                   deps=[vv], inc="dve")
        l1 = kb.op("scalar", lambda e, r: e.activation(p.stat[:, 6:7], p.stat[:, 8:9], AF.Ln),
                   deps=[ve], inc="act")
        l2 = kb.op("scalar", lambda e, r: e.activation(p.stat[:, 7:8], p.stat[:, 6:7], AF.Exp, scale=-0.5),
                   deps=[l1], inc="act")
        nrm = kb.op("vector", lambda e, r: e.tensor_scalar(p.lntok[:, :], p.xt[:, :], p.stat[:, 2:3],
                                                           p.stat[:, 7:8], ALU.subtract, ALU.mult),
                    deps=[l2], inc="dve")
        xt_busy = nrm
        for d4 in range(8):
            trs = []
            for k in range(4):
                trs.append(kb.op("tensor", lambda e, r, d4=d4, k=k: e.transpose(
                    p.psB[0][:, k * 128:(k + 1) * 128],
                    p.lntok[:, (d4 * 4 + k) * 128:(d4 * 4 + k + 1) * 128], p.c_ident[:, :]),
                    deps=[nrm, psb_busy], inc=("pe" if k == 3 else None)))
            cp = kb.op("vector", lambda e, r, d4=d4, tt=tt: e.tensor_copy(
                p.A1.ap().rearrange("p (dm t) -> p dm t", dm=32)[:, d4 * 4:d4 * 4 + 4,
                                                                tt * 128:(tt + 1) * 128],
                s3(p.psB[0][:, 0:512], 4)), deps=[trs[-1]], inc="dve")
            psb_busy = cp
    return [kb.tok("dve"), kb.tok("pe")]




def phase_sub1(kb, p):
    ln_done = emit_ln(kb, p, p.x)
    t = dma(kb, "gpsimd", r3(p.ag1_in.ap()), s3(p.A1[:, :], 32), deps=ln_done, slot="ag1s")
    ag = kb.op("gpsimd", lambda e, r: e.collective_compute(
        "AllGather", ALU.bypass, replica_groups=[list(range(NC))],
        ins=[p.ag1_in.ap()], outs=[p.ag1_out.ap()]), deps=[t], inc="cc")
    p.ag1 = ag
    if "d_ln1ag" in p.dbg_t:
        dump_dram(kb, p, p.dbg_t["d_ln1ag"], p.ag1_out, [ag])
    if SUB1 >= 1:
        attn_a_proj(kb, p)
    if SUB1 >= 2:
        attn_a_attend(kb, p)
    if SUB1 >= 3:
        attn_a_wo(kb, p)
    else:
        p.sub1_done = kb.snap()
    if "d_x1" in p.dbg_t:
        dump_dram(kb, p, p.dbg_t["d_x1"], p.xw, kb.snap())


def attn_a_proj(kb, p):
    kb.fence()
    wq_off = 0
    wk_off, wv_off = 0, 32 * 256
    wld = [dma(kb, "gpsimd", s3(p.W1[:, 0:32 * 512], 32), r3(p.wq_a.ap()), deps=p.setup, slot="pwq"),
           dma(kb, "gpsimd", s3(p.W2[:, wk_off:wk_off + 32 * 256], 32), r3(p.wk_a.ap()), deps=p.setup,
               slot="pwkv"),
           dma(kb, "gpsimd", s3(p.W2[:, wv_off:wv_off + 32 * 256], 32), r3(p.wv_a.ap()), deps=p.setup,
               slot="pwkv")]
    rhs_cons = [None, None]
    ev_busy = {}
    stg_busy = {"pq": None, "v": None}
    for tc in range(16):
        half = tc % 2
        r, colb = chunk_owner(tc)
        src = r3(p.ag1_out.ap()[D * r:D * (r + 1), colb:colb + 512])
        ld = dma(kb, "sync", s3(p.A1[:, half * 16384:(half + 1) * 16384], 32), src,
                 deps=[p.ag1, rhs_cons[half]] + wld, slot=f"prh{half}")
        for kind, nt, W, woff, wstr, bcol, sc, bank0 in (
                ("q", 4, p.W1, wq_off, 512, p.c_bq, float(ISCALE), 0),
                ("k", 2, p.W2, wk_off, 256, p.c_bk, 1.0, 1),
                ("v", 2, p.W2, wv_off, 256, p.c_bv, 1.0, 2)):
            for ot in range(nt):
                bank = bank0 + 3 * (ot % 2)
                mm = None
                for dm in range(32):
                    mm = kb.op("tensor", lambda e, r_, dm=dm, ot=ot, woff=woff, wstr=wstr, W=W, half=half,
                               bank=bank: e.matmul(
                        p.psF[bank][:, :],
                        W[:, woff + dm * wstr + ot * 128:woff + dm * wstr + ot * 128 + 128],
                        p.A1[:, half * 16384 + dm * 512:half * 16384 + (dm + 1) * 512],
                        start=(dm == 0), stop=(dm == 31)),
                        deps=([ld, ev_busy.get(bank)] if dm == 0 else []),
                        inc=("pe" if dm == 31 else None))
                if kind == "v":
                    ev = kb.op("vector", lambda e, r_, bank=bank, bcol=bcol, ot=ot: e.tensor_scalar(
                        p.pm_sb[:, 0:512], p.psF[bank][:, :], bcol[:, ot:ot + 1], None, ALU.add),
                        deps=[mm, stg_busy["v"]], inc="dve")
                    ev_busy[bank] = ev
                    trs = []
                    for k in range(4):
                        trs.append(kb.op("tensor", lambda e, r_, k=k: e.transpose(
                            p.psB[1][:, k * 128:(k + 1) * 128], p.pm_sb[:, k * 128:(k + 1) * 128],
                            p.c_ident[:, :]), deps=[ev, p.g("avcp2")], inc=("pe" if k == 3 else None)))
                    cp = kb.op("vector", lambda e, r_: e.tensor_copy(p.pt_sb[:, 0:512], p.psB[1][:, 0:512]),
                               deps=[trs[-1], p.g("avst")], inc="dve")
                    p.avcp2 = cp
                    st = dma(kb, "gpsimd",
                             r3(p.va_d.ap()[tc * 512:(tc + 1) * 512, ot * 128:(ot + 1) * 128]),
                             s3(p.pt_sb[:, 0:512], 4), deps=[cp], slot="pvst")
                    p.avst = st
                    stg_busy["v"] = st
                else:
                    if sc == 1.0:
                        ev = kb.op("vector", lambda e, r_, bank=bank, bcol=bcol, ot=ot: e.tensor_scalar(
                            p.pt_sb[:, 512:1024], p.psF[bank][:, :], bcol[:, ot:ot + 1], None,
                            ALU.add), deps=[mm, stg_busy["pq"]], inc="dve")
                    else:
                        ev = kb.op("vector", lambda e, r_, bank=bank, bcol=bcol, ot=ot, sc=sc: e.tensor_scalar(
                            p.pt_sb[:, 512:1024], p.psF[bank][:, :], bcol[:, ot:ot + 1], sc,
                            ALU.add, ALU.mult), deps=[mm, stg_busy["pq"]], inc="dve")
                    ev_busy[bank] = ev
                    outd = p.qa_d if kind == "q" else p.ka_d
                    st = dma(kb, "sync", outd.ap()[ot * 128:(ot + 1) * 128, tc * 512:(tc + 1) * 512],
                             p.pt_sb[:, 512:1024], deps=[ev], slot="pqst")
                    stg_busy["pq"] = st
        rhs_cons[half] = kb.tok("pe")
    p.aproj_done = kb.snap()
    for nm, d in (("d_qa", p.qa_d), ("d_ka", p.ka_d), ("d_va", p.va_d)):
        if nm in p.dbg_t:
            dump_dram(kb, p, p.dbg_t[nm], d, p.aproj_done)


def attn_a_attend(kb, p):
    kb.fence()
    if PHASES >= 3:
        deq_prep(kb, p)
    KOFF = [0, 8192]
    VOFF = [16384, 24576]
    kvtok = {}
    qcons = [None, None]
    for hi in range(ATT_H):
        gi = 0 if hi < 2 else 1
        qh = hi % 2
        qld = dma(kb, "sync", p.W1[:, qh * 8192:(qh + 1) * 8192], p.qa_d.ap()[hi * 128:(hi + 1) * 128, :],
                  deps=p.aproj_done + [qcons[qh]], slot=f"aq{qh}")
        if hi in (0, 2):
            kld = dma(kb, "sync", p.A1[:, KOFF[gi]:KOFF[gi] + 8192],
                      p.ka_d.ap()[gi * 128:(gi + 1) * 128, :], deps=p.aproj_done, slot=f"ak{gi}")
            if SKIP & 1:
                vld = dma(kb, "sync", s3(p.A1[:, VOFF[gi]:VOFF[gi] + 8192], 64),
                          r3(p.va_d.ap()[:, gi * 128:(gi + 1) * 128]), deps=p.aproj_done,
                          slot=f"av{gi}")
                k2 = kb.op("vector", lambda e, r, gi=gi: e.memset(p.kmx[:, gi:gi + 1], 25.0),
                           deps=[], inc="dve")
                p.akmcp = k2
                p.abnd_sr = k2
                kvtok[gi] = (kld, vld, k2)
                kld, vld, k2 = kvtok[gi]
                do_kmax = False
            else:
                do_kmax = True
        else:
            do_kmax = False
        if do_kmax:
            vld = dma(kb, "sync", s3(p.A1[:, VOFF[gi]:VOFF[gi] + 8192], 64),
                      r3(p.va_d.ap()[:, gi * 128:(gi + 1) * 128]), deps=p.aproj_done, slot=f"av{gi}")
            fb = p.W2
            sqs = []
            for i in range(4):
                sqs.append(kb.op("scalar", lambda e, r, i=i, gi=gi: e.activation(
                    fb[:, i * 2048:(i + 1) * 2048], p.A1[:, KOFF[gi] + i * 2048:KOFF[gi] + (i + 1) * 2048],
                    AF.Square), deps=[kld, p.g("akmsq"), p.g("akmnm")], inc="act"))
            p.akmsq = sqs[-1]
            nmm = None
            for kt in range(64):
                nmm = kb.op("tensor", lambda e, r, kt=kt: e.matmul(
                    p.psF[5][:, kt:kt + 1], fb[:, kt * 128:(kt + 1) * 128], p.c_onesc[:, :],
                    start=True, stop=True), deps=[sqs[kt // 16], p.g("akmcp")],
                    inc=("pe" if kt == 63 else None))
            p.akmnm = nmm
            cp = kb.op("vector", lambda e, r: e.tensor_copy(p.evs[:, 0:64], p.psF[5][:, 0:64]),
                       deps=[nmm], inc="dve")
            mx = kb.op("vector", lambda e, r: e.reduce_max(p.kmx[:, 8:9], p.evs[:, 0:64], axis=AX.X),
                       deps=[cp], inc="dve")
            tr = kb.op("tensor", lambda e, r: e.transpose(p.psF[5][0:1, 64:192], p.kmx[:, 8:9],
                                                          p.c_identf[:, :]), deps=[mx], inc="pe")
            cp2 = kb.op("vector", lambda e, r: e.tensor_copy(p.row[0:1, 0:128], p.psF[5][0:1, 64:192]),
                        deps=[tr], inc="dve")
            mx2 = kb.op("vector", lambda e, r: e.reduce_max(p.row[0:1, 128:129], p.row[0:1, 0:128],
                                                            axis=AX.X), deps=[cp2], inc="dve")
            cb = kb.op("vector", lambda e, r: e.tensor_copy(p.rowb[0:1, 0:1], p.row[0:1, 128:129]),
                       deps=[mx2], inc="dve")
            bc = kb.op("tensor", lambda e, r: e.matmul(p.psF[5][:, 192:193], p.c_onesr[0:1, 0:128],
                                                       p.rowb[0:1, 0:1], start=True, stop=True),
                       deps=[cb], inc="pe")
            k2 = kb.op("vector", lambda e, r, gi=gi: e.tensor_copy(p.kmx[:, gi:gi + 1],
                                                                   p.psF[5][:, 192:193]),
                       deps=[bc], inc="dve")
            p.akmcp = k2
            p.abnd_sr = k2
            kvtok[gi] = (kld, vld, k2)
        kld, vld, k2 = kvtok[gi]

        for qc in range(ATT_QC):
            if PHASES >= 3:
                deq_step(kb, p, 5)
            if SKIP & 2:
                bn = kb.op("vector", lambda e, r, qc=qc: e.memset(
                    p.bnd[:, (qc % 2) * 4:(qc % 2) * 4 + 4], -30.0),
                    deps=[qld, p.g("albn")], inc="dve")
            else:
                sq = kb.op("scalar", lambda e, r, qh=qh, qc=qc: e.activation(
                    p.pm_sb[:, 0:512], p.W1[:, qh * 8192 + qc * 512:qh * 8192 + (qc + 1) * 512], AF.Square),
                    deps=[qld, p.g("aepi_tr"), p.g("abnd_sr")], inc="act")
                bm = None
                for qt in range(4):
                    bm = kb.op("tensor", lambda e, r, qt=qt: e.matmul(
                        p.psF[5][:, 256 + qt:257 + qt], p.pm_sb[:, qt * 128:(qt + 1) * 128],
                        p.c_onesc[:, :],
                        start=True, stop=True), deps=[sq, p.g("abnd_sr")],
                        inc=("pe" if qt == 3 else None))
                sr = kb.op("scalar", lambda e, r, gi=gi: e.activation(
                    p.lstg[:, 0:4], p.psF[5][:, 256:260], AF.Sqrt, scale=p.kmx[:, gi:gi + 1]),
                    deps=[bm, k2], inc="act")
                p.abnd_sr = sr
                bn = kb.op("vector", lambda e, r, qc=qc: e.tensor_scalar(
                    p.bnd[:, (qc % 2) * 4:(qc % 2) * 4 + 4], p.lstg[:, 0:4], -1.0, None, ALU.mult),
                    deps=[sr, p.g("albn")], inc="dve")
            lrst = kb.op("vector", lambda e, r: e.memset(p.lstg[:, 8:12], 0.0),
                         deps=[p.g("alrc")], inc="dve")
            lacc = lrst
            pvl = None
            exp_busy = [None] * 4
            if SKIP & 4:
                lacc = kb.op("vector", lambda e, r: e.memset(p.lstg[:, 8:12], 1.0),
                             deps=[lrst], inc="dve")
                pvl = kb.op("vector", lambda e, r: e.memset(p.psF[4][:, :], 1.0),
                            deps=[p.g("aoev"), lacc], inc="dve")
            for kc in range(16 if not (SKIP & 4) else 0):
                for qt in range(4):
                    smm = kb.op("tensor", lambda e, r, qt=qt, kc=kc, qh=qh, qc=qc, gi=gi: e.matmul(
                        p.psF[qt][:, :],
                        p.W1[:, qh * 8192 + qc * 512 + qt * 128:qh * 8192 + qc * 512 + (qt + 1) * 128],
                        p.A1[:, KOFF[gi] + kc * 512:KOFF[gi] + (kc + 1) * 512],
                        start=True, stop=True),
                        deps=[kld, bn, exp_busy[qt]] + ([p.g("aoev")] if kc == 0 and qt == 0 else []),
                        inc="pe")
                    ex = kb.op("scalar", lambda e, r, qt=qt, qc=qc: e.activation(
                        p.p_sb[:, qt * 512:(qt + 1) * 512], p.psF[qt][:, :], AF.Exp,
                        bias=p.bnd[:, (qc % 2) * 4 + qt:(qc % 2) * 4 + qt + 1],
                        accum_out=p.lstg[:, 4 + qt:5 + qt]),
                        deps=[smm, lacc, p.g("aptcp_3")], inc="act")
                    exp_busy[qt] = ex
                lacc = kb.op("vector", lambda e, r: e.tensor_tensor(
                    p.lstg[:, 8:12], p.lstg[:, 8:12], p.lstg[:, 4:8], ALU.add),
                    deps=[exp_busy[3], lacc], inc="dve")
                if SKIP & 16:
                    if kc == 15:
                        pvl = kb.op("vector", lambda e, r: e.memset(p.psF[4][:, :], 1.0),
                                    deps=[p.g("aoev"), lacc], inc="dve")
                    continue
                cps = []
                for k in range(4):
                    bsel = k % 2
                    trs = None
                    for qt in range(4):
                        trs = kb.op("tensor", lambda e, r, k=k, qt=qt, bsel=bsel: e.transpose(
                            p.psB[bsel][:, (k // 2) * 512 + qt * 128:(k // 2) * 512 + (qt + 1) * 128],
                            p.p_sb[:, qt * 512 + k * 128:qt * 512 + k * 128 + 128], p.c_ident[:, :]),
                            deps=[exp_busy[qt], p.g(f"apcb_{bsel}")], inc=("pe" if qt == 3 else None))
                    cp = kb.op("vector", lambda e, r, k=k, bsel=bsel: e.tensor_copy(
                        p.pt_sb[:, k * 512:(k + 1) * 512],
                        p.psB[bsel][:, (k // 2) * 512:(k // 2) * 512 + 512]), deps=[trs], inc="dve")
                    cps.append(cp)
                    p.__dict__[f"aptcp_{k}"] = cp
                    p.__dict__[f"apcb_{bsel}"] = cp
                if SKIP & 32:
                    if kc == 15:
                        pvl = kb.op("vector", lambda e, r: e.memset(p.psF[4][:, :], 1.0),
                                    deps=[p.g("aoev"), cps[3]], inc="dve")
                    continue
                for k in range(4):
                    pvl = kb.op("tensor", lambda e, r, k=k, kc=kc, gi=gi: e.matmul(
                        p.psF[4][:, :],
                        p.A1[:, VOFF[gi] + (kc * 4 + k) * 128:VOFF[gi] + (kc * 4 + k + 1) * 128],
                        p.pt_sb[:, k * 512:(k + 1) * 512],
                        start=(kc == 0 and k == 0), stop=(kc == 15 and k == 3)),
                        deps=[cps[k], vld], inc=("pe" if k == 3 else None))
            if SKIP & 8:
                dummy = kb.op("vector", lambda e, r: e.tensor_copy(p.pm_sb[:, 1536:2048],
                                                                   p.pm_sb[:, 1024:1536]),
                              deps=[pvl, p.g("aost")], inc="dve")
                p.aoev = dummy
                p.alrc = dummy
                st = dma(kb, "gpsimd",
                         r3(p.oa_d.ap()[hi * 8192 + qc * 512:hi * 8192 + (qc + 1) * 512, :]),
                         s3(p.pm_sb[:, 1536:2048], 4), deps=[dummy], slot="aost")
                p.aost = st
                p.albn = bn
                continue
            rc = kb.op("vector", lambda e, r: e.reciprocal(p.lstg[:, 12:16], p.lstg[:, 8:12]),
                       deps=[lacc], inc="dve")
            p.alrc = rc
            oev = kb.op("vector", lambda e, r: e.tensor_copy(p.pm_sb.ap()[:, 1024:1536], p.psF[4][:, :]),
                        deps=[pvl, p.g("aepi_tr")], inc="dve")
            p.aoev = oev
            otr = None
            for qt in range(4):
                otr = kb.op("tensor", lambda e, r, qt=qt: e.transpose(
                    p.psB[0][:, qt * 128:(qt + 1) * 128],
                    p.pm_sb[:, 1024 + qt * 128:1024 + (qt + 1) * 128], p.c_ident[:, :]),
                    deps=[oev, p.g("apcb_0"), p.g("aosc")],
                    inc=("pe" if qt == 3 else None))
            p.aepi_tr = otr
            oscs = []
            for qt in range(4):
                oscs.append(kb.op("vector", lambda e, r, qt=qt: e.tensor_scalar(
                    p.pm_sb[:, 1536 + qt * 128:1536 + (qt + 1) * 128],
                    p.psB[0][:, qt * 128:(qt + 1) * 128],
                    p.lstg[:, 12 + qt:13 + qt], None, ALU.mult), deps=[otr, rc, p.g("aost")], inc="dve"))
            p.aosc = oscs[-1]
            p.apcb_0 = oscs[-1]
            st = dma(kb, "gpsimd",
                     r3(p.oa_d.ap()[hi * 8192 + qc * 512:hi * 8192 + (qc + 1) * 512, :]),
                     s3(p.pm_sb[:, 1536:2048], 4), deps=oscs, slot="aost")
            p.aost = st
            p.albn = exp_busy[3]
            if PHASES >= 3:
                deq_step(kb, p, 5)
        qcons[qh] = kb.tok("pe")
    if PHASES >= 3:
        deq_step(kb, p, 0)
    p.aatt_done = kb.snap()
    if "d_oa" in p.dbg_t:
        dump_dram(kb, p, p.dbg_t["d_oa"], p.oa_d, p.aatt_done)


def attn_a_wo(kb, p):
    kb.fence()
    old = None
    for i in range(4):
        old = dma(kb, "gpsimd", s3(p.A1[:, i * 8192:(i + 1) * 8192], 64),
                  r3(p.oa_d.ap()[i * 8192:(i + 1) * 8192, :]), deps=p.aatt_done, slot="wold")
    wcons = [None, None]
    ev_busy = [None, None]
    xst = {}
    for oc in range(8):
        half = oc % 2
        W = p.W1 if half == 0 else p.W2
        wld = dma(kb, "gpsimd", s3(W[:, 0:32 * 512], 32),
                  r3(p.wo_a.ap()[:, oc * 512:(oc + 1) * 512]),
                  deps=list(p.setup) + [wcons[half]] + (p.aatt_done if oc < 2 else []), slot=f"wow{half}")
        p.__dict__[f"brld_{half}"] = dma(kb, "gpsimd", p.brow[0:1, half * 512:half * 512 + 512],
                                         p.bo_a.ap()[0:1, oc * 512:(oc + 1) * 512],
                                         deps=[wcons[half]], slot=f"wob{half}")
        for hj in range(8):
            hi, j = hj // 2, hj % 2
            bank = hj % 2
            mm = None
            for qp in range(32):
                mm = kb.op("tensor", lambda e, r, hi=hi, j=j, qp=qp, W=W, bank=bank: e.matmul(
                    p.psF[bank][:, :],
                    p.A1[:, (hi * 64 + j * 32 + qp) * 128:(hi * 64 + j * 32 + qp + 1) * 128],
                    W[:, qp * 512:(qp + 1) * 512], start=(qp == 0), stop=False),
                    deps=([old, wld, ev_busy[bank]] if qp == 0 else []), inc=None)
            mm = kb.op("tensor", lambda e, r, oc=oc, bank=bank, half=half: e.matmul(
                p.psF[bank][:, :], p.c_onesr[0:1, 0:128], p.brow[0:1, half * 512:half * 512 + 512],
                start=False, stop=True), deps=[p.g(f"brld_{half}")], inc="pe")
            co = bank * 512
            xld = dma(kb, "sync", p.evs[:, co:co + 512],
                      p.x.ap()[hj * 128:(hj + 1) * 128, oc * 512:(oc + 1) * 512],
                      deps=[ev_busy[bank]], slot=f"wx{bank}")
            ev = kb.op("vector", lambda e, r, bank=bank, co=co: e.tensor_tensor(
                p.evs2[:, co:co + 512], p.psF[bank][:, :], p.evs[:, co:co + 512], ALU.add),
                deps=[mm, xld, ev_busy[bank], xst.get(bank)], inc="dve")
            ev_busy[bank] = ev
            st = dma(kb, "sync", p.xw.ap()[hj * 128:(hj + 1) * 128, oc * 512:(oc + 1) * 512],
                     p.evs2[:, co:co + 512], deps=[ev], slot=f"wxs{bank}")
            xst[bank] = st
        wcons[half] = kb.tok("pe")
    p.sub1_done = kb.snap()




def phase_ffn(kb, p, which):
    prev = p.sub1_done if which == 1 else p.sub3_done
    src = p.xw
    dst = p.xw if which == 1 else p.out
    ln_done = emit_ln(kb, p, src, extra_deps=prev)

    lg_busy = None
    w_done = []
    for tt in range(8):
        mm = None
        for dm in range(32):
            mm = kb.op("tensor", lambda e, r, dm=dm, tt=tt: e.matmul(
                p.psF[2][:, 0:8], p.A1[:, dm * 1024 + tt * 128:dm * 1024 + (tt + 1) * 128],
                p.c_wrT[:, dm * 8:(dm + 1) * 8], start=(dm == 0), stop=(dm == 31)),
                deps=(ln_done + [lg_busy] if dm == 0 else []), inc=("pe" if dm == 31 else None))
        lg = kb.op("vector", lambda e, r: e.tensor_copy(p.evs[:, 0:8], p.psF[2][:, 0:8]),
                   deps=[mm], inc="dve")
        m1 = kb.op("vector", lambda e, r: e.reduce_max(p.stat[:, 0:1], p.evs[:, 0:8], axis=AX.X),
                   deps=[lg], inc="dve")
        e1 = kb.op("vector", lambda e, r: e.tensor_scalar(p.evs[:, 8:16], p.evs[:, 0:8], p.stat[:, 0:1],
                                                          None, ALU.is_ge), deps=[m1], inc="dve")
        t2 = kb.op("vector", lambda e, r: e.scalar_tensor_tensor(p.evs[:, 16:24], p.evs[:, 8:16], -1e30,
                                                                 p.evs[:, 0:8], ALU.mult, ALU.add),
                   deps=[e1], inc="dve")
        m2 = kb.op("vector", lambda e, r: e.reduce_max(p.stat[:, 1:2], p.evs[:, 16:24], axis=AX.X),
                   deps=[t2], inc="dve")
        e2 = kb.op("vector", lambda e, r: e.tensor_scalar(p.evs[:, 24:32], p.evs[:, 16:24],
                                                          p.stat[:, 1:2], None, ALU.is_ge),
                   deps=[m2], inc="dve")
        df = kb.op("vector", lambda e, r: e.tensor_tensor(p.stat[:, 2:3], p.stat[:, 1:2], p.stat[:, 0:1],
                                                          ALU.subtract), deps=[e2], inc="dve")
        eb = kb.op("scalar", lambda e, r: e.activation(p.stat[:, 3:4], p.stat[:, 2:3], AF.Exp),
                   deps=[df], inc="act")
        dn = kb.op("vector", lambda e, r: e.tensor_scalar(p.stat[:, 4:5], p.stat[:, 3:4], 1.0, None,
                                                          ALU.add), deps=[eb], inc="dve")
        rcp = kb.op("vector", lambda e, r: e.reciprocal(p.stat[:, 5:6], p.stat[:, 4:5]),
                    deps=[dn], inc="dve")
        w1 = kb.op("vector", lambda e, r: e.tensor_scalar(p.stat[:, 6:7], p.stat[:, 5:6], 2.0, None,
                                                          ALU.mult), deps=[rcp], inc="dve")
        w2 = kb.op("vector", lambda e, r: e.tensor_tensor(p.stat[:, 7:8], p.stat[:, 3:4], p.stat[:, 6:7],
                                                          ALU.mult), deps=[w1], inc="dve")
        s1 = kb.op("vector", lambda e, r: e.tensor_scalar(p.evs[:, 8:16], p.evs[:, 8:16], p.stat[:, 6:7],
                                                          None, ALU.mult), deps=[w2], inc="dve")
        s2 = kb.op("vector", lambda e, r: e.tensor_scalar(p.evs[:, 24:32], p.evs[:, 24:32],
                                                          p.stat[:, 7:8], None, ALU.mult),
                   deps=[s1], inc="dve")
        wf = kb.op("vector", lambda e, r: e.tensor_tensor(p.pm_sb[:, 0:8], p.evs[:, 8:16],
                                                          p.evs[:, 24:32], ALU.add), deps=[s2], inc="dve")
        tr = kb.op("tensor", lambda e, r: e.transpose(p.psB[1][0:8, 0:128], p.pm_sb[:, 0:8],
                                                      p.c_ident[:, :]), deps=[wf, lg_busy], inc="pe")
        cp = kb.op("vector", lambda e, r, tt=tt: e.tensor_copy(p.wrow[0:8, tt * 128:(tt + 1) * 128],
                                                               p.psB[1][0:8, 0:128]), deps=[tr], inc="dve")
        lg_busy = cp
        w_done.append(cp)
    ex1 = kb.op("tensor", lambda e, r: e.matmul(p.psF[2][:, :], p.c_emat[:, :], p.wrow[0:8, 0:512],
                                                start=True, stop=True), deps=w_done, inc="pe")
    ex2 = kb.op("tensor", lambda e, r: e.matmul(p.psF[3][:, :], p.c_emat[:, :], p.wrow[0:8, 512:1024],
                                                start=True, stop=True), deps=[], inc="pe")
    wx1 = kb.op("vector", lambda e, r: e.tensor_copy(p.wexp[:, 0:512], p.psF[2][:, :]),
                deps=[ex1], inc="dve")
    wx2 = kb.op("vector", lambda e, r: e.tensor_copy(p.wexp[:, 512:1024], p.psF[3][:, :]),
                deps=[ex2], inc="dve")

    acld = dma(kb, "gpsimd", s3(p.W2[:, 0:4096], 32), r3(p.acT.ap()),
               deps=ln_done + list(p.setup), slot="fac")
    hT_done = []
    for tc in range(2):
        mm = None
        for dm in range(32):
            mm = kb.op("tensor", lambda e, r, dm=dm, tc=tc: e.matmul(
                p.psF[2 + tc][:, :], p.W2[:, dm * 128:(dm + 1) * 128],
                p.A1[:, dm * 1024 + tc * 512:dm * 1024 + (tc + 1) * 512],
                start=(dm == 0), stop=(dm == 31)),
                deps=([wx1, wx2, acld] if dm == 0 else []), inc=("pe" if dm == 31 else None))
        hw = kb.op("vector", lambda e, r, tc=tc: e.tensor_tensor(
            p.hT[:, tc * 512:(tc + 1) * 512], p.psF[2 + tc][:, :], p.wexp[:, tc * 512:(tc + 1) * 512],
            ALU.mult), deps=[mm], inc="dve")
        hT_done.append(hw)

    if which == 1 and "d_hT" in p.dbg_t:
        cpd = kb.op("vector", lambda e, r: e.tensor_copy(p.evs[:, 0:1024], p.hT[:, :]),
                    deps=hT_done, inc="dve")
        dma(kb, "gpsimd", p.dbg_t["d_hT"].ap(), p.evs[:, 0:1024], deps=[cpd], slot="dbg")
    if which == 1 and "d_wexp" in p.dbg_t:
        cpw = kb.op("vector", lambda e, r: e.tensor_copy(p.evs[:, 0:1024], p.wexp[:, :]),
                    deps=[wx1, wx2] + (hT_done if True else []), inc="dve")
        dma(kb, "gpsimd", p.dbg_t["d_wexp"].ap(), p.evs[:, 0:1024], deps=[cpw], slot="dbg")
    kb.fence(("sync", "vector", "tensor"))
    wcons = [None, None]
    ev_busy = [None, None]
    xst = {}
    semname = None
    for oc in range(8):
        half = oc % 2
        W = p.W1 if half == 0 else p.W2
        wld = dma(kb, "gpsimd", s3(W[:, 0:32 * 512], 32),
                  r3(p.wbT.ap()[:, oc * 512:(oc + 1) * 512]),
                  deps=list(p.setup) + [wcons[half]] + (prev if oc < 2 else []) +
                       (hT_done if oc < 2 else []), slot=f"fw{half}")
        p.__dict__[f"fbrld_{half}"] = dma(kb, "gpsimd", p.brow[0:1, half * 512:half * 512 + 512],
                                          p.bb.ap()[0:1, oc * 512:(oc + 1) * 512],
                                          deps=[wcons[half]] + (hT_done if oc < 2 else []),
                                          slot=f"fb{half}")
        for tt in range(8):
            bank = tt % 2
            mm = None
            for dm in range(32):
                mm = kb.op("tensor", lambda e, r, dm=dm, tt=tt, W=W, bank=bank: e.matmul(
                    p.psF[bank][:, :], p.A1[:, dm * 1024 + tt * 128:dm * 1024 + (tt + 1) * 128],
                    W[:, dm * 512:(dm + 1) * 512], start=(dm == 0), stop=False),
                    deps=([wld, ev_busy[bank]] + hT_done if dm == 0 else []), inc=None)
            kb.op("tensor", lambda e, r, oc=oc, bank=bank, half=half: e.matmul(
                p.psF[bank][:, :], p.c_onesr[0:1, 0:128], p.brow[0:1, half * 512:half * 512 + 512],
                start=False, stop=False), deps=[p.g(f"fbrld_{half}")], inc=None)
            mm = kb.op("tensor", lambda e, r, oc=oc, tt=tt, bank=bank: e.matmul(
                p.psF[bank][:, :], p.hT[:, tt * 128:(tt + 1) * 128],
                p.c_bcat[:, oc * 512:(oc + 1) * 512], start=False, stop=True), deps=[], inc="pe")
            co = bank * 512
            xld = dma(kb, "sync", p.evs[:, co:co + 512],
                      src.ap()[tt * 128:(tt + 1) * 128, oc * 512:(oc + 1) * 512],
                      deps=[ev_busy[bank]] + (prev if oc < 2 else []), slot=f"fx{bank}")
            ev = kb.op("vector", lambda e, r, bank=bank, co=co: e.tensor_tensor(
                p.evs2[:, co:co + 512], p.psF[bank][:, :], p.evs[:, co:co + 512], ALU.add),
                deps=[mm, xld, ev_busy[bank], xst.get(bank)], inc="dve")
            ev_busy[bank] = ev
            st = dma(kb, "sync", dst.ap()[tt * 128:(tt + 1) * 128, oc * 512:(oc + 1) * 512],
                     p.evs2[:, co:co + 512], deps=[ev],
                     slot=("fs3" if which == 3 else "fs1") + str(bank), sem=semname)
            xst[bank] = st
        wcons[half] = kb.tok("pe")
    done = kb.snap()
    if which == 1:
        p.ffn1_done = done
        if "d_x2" in p.dbg_t:
            dump_dram(kb, p, p.dbg_t["d_x2"], p.xw, done)




def phase_sub3(kb, p):
    deq_finish(kb, p)
    ln_done = emit_ln(kb, p, p.xw, extra_deps=p.ffn1_done + p.deq_done)
    attn_b_proj(kb, p, ln_done)
    attn_b_attend(kb, p)
    attn_b_oproj(kb, p)
    if "d_x3" in p.dbg_t:
        dump_dram(kb, p, p.dbg_t["d_x3"], p.xw, p.sub3_done)


def deq_prep(kb, p):
    p.dq_units = []
    for qw, sc, outd, N in ((p.qwq, p.scq, p.wdq_d, D), (p.qwo, p.sco, p.wdo_d, D),
                            (p.qwk, p.sck, p.wdk_d, KV), (p.qwv, p.scv, p.wdv_d, KV)):
        for g in range(32):
            for u in range(N // 512):
                p.dq_units.append((qw, sc, outd, g, u))
    p.dq_iss = 0
    p.dq_cmp = 0
    p.dq_busy = [None] * 8
    p.dq_stb = [None] * 8
    p.dq_ld = [None] * 8
    p.dq_bc = [None] * 8
    p.dq_dep0 = kb.snap()


def deq_step(kb, p, n):
    while p.dq_cmp < p.dq_iss:
        i = p.dq_cmp
        qw, sc, outd, g, u = p.dq_units[i]
        sl = i % 8
        ci = 8192 + sl * 512
        cs = 12288 + sl * 512
        t2 = kb.op("vector", lambda e, r, ci=ci, cs=cs, sl=sl: e.scalar_tensor_tensor(
            p.lntok[:, sl * 512:sl * 512 + 512], p.W2[:, ci:ci + 512], -8.0,
            p.W2[:, cs:cs + 512], ALU.add, ALU.mult),
            deps=[p.dq_bc[sl], p.dq_stb[sl]], inc="dve")
        st = dma(kb, "sync", outd.ap()[g * 128:(g + 1) * 128, u * 512:(u + 1) * 512],
                 p.lntok[:, sl * 512:sl * 512 + 512], deps=[t2], slot=f"dqs{sl}")
        p.dq_busy[sl] = t2
        p.dq_stb[sl] = st
        p.dq_cmp += 1
    for _ in range(n):
        if p.dq_iss >= len(p.dq_units):
            return
        i = p.dq_iss
        qw, sc, outd, g, u = p.dq_units[i]
        sl = i % 8
        dep0 = p.dq_dep0 if i < 8 else []
        p.dq_ld[sl] = dma(kb, "gpsimd", p.W2[:, 8192 + sl * 512:8192 + sl * 512 + 512],
                          qw.ap()[g * 128:(g + 1) * 128, u * 512:(u + 1) * 512],
                          deps=[p.dq_busy[sl]] + dep0, slot=f"dqi{sl}")
        p.dq_bc[sl] = dma(kb, "gpsimd", p.W2[:, 12288 + sl * 512:12288 + sl * 512 + 512],
                          sc.ap()[g:g + 1, u * 512:(u + 1) * 512].partition_broadcast(128),
                          deps=[p.dq_busy[sl]] + dep0, slot=f"dqi{sl}")
        p.dq_iss += 1


def deq_finish(kb, p):
    if not hasattr(p, "dq_iss"):
        deq_prep(kb, p)
    while p.dq_cmp < len(p.dq_units):
        deq_step(kb, p, len(p.dq_units) - p.dq_iss)
    p.deq_done = [t for t in p.dq_busy + p.dq_stb if t is not None]
    for nm, d in (("d_wdq", p.wdq_d), ("d_wdo", p.wdo_d)):
        if nm in p.dbg_t:
            dump_dram(kb, p, p.dbg_t[nm], d, p.deq_done)


def attn_b_proj(kb, p, ln_done):
    W2f = p.W2.ap().bitcast(F32)
    rope_ld = []
    for i, nm in enumerate(("ropekc", "ropeks", "ropeqc", "ropeqs")):
        rope_ld.append(dma(kb, "sync", W2f[:, i * 1024:(i + 1) * 1024], getattr(p, nm).ap(),
                           deps=list(p.setup) + p.deq_done + list(ln_done), slot="brl"))

    def rope_pair2(mm_dep, bank, tc, is_q, out_view, outdep=()):
        cv = kb.op("vector", lambda e, r, bank=bank: e.tensor_copy(p.evs[:, 0:512], p.psF[bank][:, :]),
                   deps=[mm_dep, p.g("rp_t1"), p.g("rp_cb")], inc="dve")
        cb = kb.op("vector", lambda e, r, bank=bank: e.tensor_copy(p.pm_sb[:, 1536:2048],
                                                                   p.psF[bank][:, :]),
                   deps=[mm_dep, p.g("rp_rot")], inc="dve")
        p.rp_cb = cb
        rot = kb.op("tensor", lambda e, r: e.matmul(p.psF[5][:, :], p.c_rot[:, :],
                                                    p.pm_sb[:, 1536:2048], start=True, stop=True),
                    deps=[cb, p.g("rp_t2")], inc="pe")
        p.rp_rot = rot
        base = 2048 if is_q else 0
        t1 = kb.op("vector", lambda e, r, tc=tc, base=base: e.tensor_tensor(
            p.evs2[:, 0:512], p.evs[:, 0:512], W2f[:, base + tc * 512:base + (tc + 1) * 512], ALU.mult),
            deps=[cv] + rope_ld, inc="dve")
        p.rp_t1 = t1
        t2 = kb.op("vector", lambda e, r, tc=tc, base=base: e.tensor_tensor(
            p.xt[:, 0:512], p.psF[5][:, :],
            W2f[:, base + 1024 + tc * 512:base + 1024 + (tc + 1) * 512], ALU.mult),
            deps=[rot, t1], inc="dve")
        p.rp_t2 = t2
        fin = kb.op("vector", lambda e, r, out_view=out_view: e.tensor_tensor(
            out_view, p.evs2[:, 0:512], p.xt[:, 0:512], ALU.add),
            deps=[t2] + list(outdep), inc="dve")
        return fin

    wcons = [None, None]
    kst = None
    vst = None
    for kind in ("k", "v"):
        wsrc = p.wdk_d if kind == "k" else p.wdv_d
        for ot in range(8):
            sl = ot % 2
            wld = dma(kb, "gpsimd", s3(p.W1[:, sl * 4096:sl * 4096 + 32 * 128], 32),
                      r3(wsrc.ap()[:, ot * 128:(ot + 1) * 128]),
                      deps=ln_done + p.deq_done + [wcons[sl]], slot=f"bw{sl}")
            for tc in range(2):
                bank = tc
                mm = None
                for dm in range(32):
                    mm = kb.op("tensor", lambda e, r, dm=dm, tc=tc, sl=sl, bank=bank: e.matmul(
                        p.psF[bank][:, :], p.W1[:, sl * 4096 + dm * 128:sl * 4096 + (dm + 1) * 128],
                        p.A1[:, dm * 1024 + tc * 512:dm * 1024 + (tc + 1) * 512],
                        start=(dm == 0), stop=(dm == 31)),
                        deps=([wld, p.g(f"bpev_{bank}")] if dm == 0 else []),
                        inc=("pe" if dm == 31 else None))
                if kind == "k":
                    fin = rope_pair2(mm, bank, tc, False, p.p_sb[:, 0:512], outdep=[kst])
                    p.__dict__[f"bpev_{bank}"] = fin
                    kst = dma(kb, "sync", p.kb_in.ap()[ot * 128:(ot + 1) * 128, tc * 512:(tc + 1) * 512],
                              p.p_sb[:, 0:512], deps=[fin], slot="bkst")
                    sqk = kb.op("scalar", lambda e, r: e.activation(p.pm_sb[:, 0:512], p.p_sb[:, 0:512],
                                                                    AF.Square), deps=[fin, p.g("bkn_mm")],
                               inc="act")
                    nm4 = None
                    for qt in range(4):
                        nm4 = kb.op("tensor", lambda e, r, qt=qt: e.matmul(
                            p.psF[4][:, qt:qt + 1], p.pm_sb[:, qt * 128:(qt + 1) * 128], p.c_onesc[:, :],
                            start=True, stop=True), deps=[sqk, p.g("bkn_cp")],
                            inc=("pe" if qt == 3 else None))
                    p.bkn_mm = nm4
                    cpn = kb.op("vector", lambda e, r: e.tensor_copy(p.lstg[:, 4:8], p.psF[4][:, 0:4]),
                                deps=[nm4], inc="dve")
                    p.bkn_cp = cpn
                    mr = kb.op("vector", lambda e, r: e.tensor_reduce(p.lstg[:, 8:9],
                                                                      p.lstg[:, 4:8], AX.X, ALU.max),
                               deps=[cpn], inc="dve")
                    km = kb.op("vector", lambda e, r, ot=ot, tc=tc: (
                        e.tensor_copy(p.kmx[:, 8 + ot:9 + ot], p.lstg[:, 8:9]) if tc == 0 else
                        e.tensor_tensor(p.kmx[:, 8 + ot:9 + ot], p.kmx[:, 8 + ot:9 + ot],
                                        p.lstg[:, 8:9], ALU.max)), deps=[mr], inc="dve")
                else:
                    cv = kb.op("scalar", lambda e, r, bank=bank: e.activation(
                        p.p_sb[:, 0:512], p.psF[bank][:, :], AF.Copy),
                        deps=[mm, p.g("bvtr"), kst], inc="act")
                    p.__dict__[f"bpev_{bank}"] = cv
                    trs = None
                    for k in range(4):
                        trs = kb.op("tensor", lambda e, r, k=k: e.transpose(
                            p.psB[1][:, k * 128:(k + 1) * 128], p.p_sb[:, k * 128:(k + 1) * 128],
                            p.c_ident[:, :]), deps=[cv, p.g("bvcp")], inc=("pe" if k == 3 else None))
                    p.bvtr = trs
                    cpv = kb.op("vector", lambda e, r: e.tensor_copy(p.pt_sb[:, 0:512], p.psB[1][:, 0:512]),
                                deps=[trs, vst], inc="dve")
                    p.bvcp = cpv
                    vst = dma(kb, "gpsimd",
                              r3(p.vb_in.ap()[tc * 512:(tc + 1) * 512, ot * 128:(ot + 1) * 128]),
                              s3(p.pt_sb[:, 0:512], 4), deps=[cpv], slot="bvst")
            wcons[sl] = kb.tok("pe")
        if kind == "k":
            kag = kb.op("gpsimd", lambda e, r: e.collective_compute(
                "AllGather", ALU.bypass, replica_groups=[list(range(NC))],
                ins=[p.kb_in.ap()], outs=[p.kb_ag.ap()]), deps=[kst], inc="cc")
    vag = kb.op("gpsimd", lambda e, r: e.collective_compute(
        "AllGather", ALU.bypass, replica_groups=[list(range(NC))],
        ins=[p.vb_in.ap()], outs=[p.vb_ag.ap()]), deps=[vst], inc="cc")

    kb.fence(("tensor", "vector", "scalar"))
    tr = kb.op("tensor", lambda e, r: e.transpose(p.psF[4][0:8, 0:128], p.kmx[:, 8:16], p.c_identf[:, :]),
               deps=[], inc="pe")
    cp = kb.op("vector", lambda e, r: e.tensor_copy(p.evs[0:8, 0:128], p.psF[4][0:8, 0:128]),
               deps=[tr], inc="dve")
    mx = kb.op("vector", lambda e, r: e.tensor_reduce(p.evs[0:8, 128:129], p.evs[0:8, 0:128], AX.X,
                                                      ALU.max), deps=[cp], inc="dve")
    tr2 = kb.op("tensor", lambda e, r: e.transpose(p.psF[4][0:1, 128:136], p.evs[0:8, 128:129],
                                                   p.c_identf[0:8, 0:8]), deps=[mx], inc="pe")
    cp2 = kb.op("vector", lambda e, r: e.tensor_copy(p.row[0:1, 0:8], p.psF[4][0:1, 128:136]),
                deps=[tr2], inc="dve")
    stk = dma(kb, "gpsimd", p.km_in.ap(), p.row[0:1, 0:8], deps=[cp2], slot="kms")

    kmr = kb.op("gpsimd", lambda e, r: e.collective_compute(
        "AllReduce", ALU.max, replica_groups=[list(range(NC))],
        ins=[p.km_in.ap()], outs=[p.km_out.ap()]), deps=[stk], inc="cc")
    p.bkv_ag = [kag, vag, kmr]
    kldm = dma(kb, "sync", p.row[0:1, 8:16], p.km_out.ap(), deps=[kmr], slot="kml")
    ksq = kb.op("scalar", lambda e, r: e.activation(p.row[0:1, 16:24], p.row[0:1, 8:16],
                                                    AF.Sqrt), deps=[kldm], inc="act")
    kng = kb.op("vector", lambda e, r: e.tensor_scalar(p.row[0:1, 24:32], p.row[0:1, 16:24],
                                                       -1.0, None, ALU.mult),
                deps=[ksq], inc="dve")
    p.bkmneg = kng
    p.bqn_sr = cp2

    wcons = [None, None]
    qst = None
    for hi in range(32):
        sl = hi % 2
        wld = dma(kb, "gpsimd", s3(p.W1[:, sl * 4096:sl * 4096 + 32 * 128], 32),
                  r3(p.wdq_d.ap()[:, hi * 128:(hi + 1) * 128]),
                  deps=ln_done + p.deq_done + [wcons[sl]], slot=f"bqw{sl}")
        for tc in range(2):
            bank = tc
            mm = None
            for dm in range(32):
                mm = kb.op("tensor", lambda e, r, dm=dm, tc=tc, sl=sl, bank=bank: e.matmul(
                    p.psF[bank][:, :], p.W1[:, sl * 4096 + dm * 128:sl * 4096 + (dm + 1) * 128],
                    p.A1[:, dm * 1024 + tc * 512:dm * 1024 + (tc + 1) * 512],
                    start=(dm == 0), stop=(dm == 31)),
                    deps=([wld, p.g(f"bpev_{bank}")] if dm == 0 else []),
                    inc=("pe" if dm == 31 else None))
            fin = rope_pair2(mm, bank, tc, True, p.p_sb[:, 0:512],
                             outdep=[qst, p.g("bvtr")])
            p.__dict__[f"bpev_{bank}"] = fin
            qst = dma(kb, "sync",
                      r3(p.qb_d.ap()[hi * 1024 + tc * 512:hi * 1024 + (tc + 1) * 512, 0:128]),
                      s3(p.p_sb[:, 0:512], 4), deps=[fin], slot="bqst")
            sqq = kb.op("scalar", lambda e, r: e.activation(p.pm_sb[:, 0:512], p.p_sb[:, 0:512],
                                                            AF.Square), deps=[fin, p.g("bqn_mm")],
                        inc="act")
            rmm = kb.op("tensor", lambda e, r: e.matmul(
                p.psF[4][0:1, 0:512], p.c_onesc[:, :], p.pm_sb[:, 0:512],
                start=True, stop=True), deps=[sqq, p.g("bqn_sr")], inc="pe")
            p.bqn_mm = rmm
            srr = kb.op("scalar", lambda e, r: e.activation(
                p.rowb[0:1, 0:512], p.psF[4][0:1, 0:512], AF.Sqrt),
                deps=[rmm, p.g("bbst")], inc="act")
            p.bqn_sr = srr
            fx = kb.op("vector", lambda e, r, hi=hi: e.tensor_scalar(
                p.rowb[0:1, 0:512], p.rowb[0:1, 0:512],
                p.row[0:1, 24 + hi // 4:25 + hi // 4], None, ALU.mult),
                deps=[srr, p.bkmneg], inc="dve")
            bst = dma(kb, "sync", p.browd.ap()[hi * 2 + tc:hi * 2 + tc + 1, :],
                      p.rowb[0:1, 0:512], deps=[fx], slot="bbst")
            p.bbst = bst
        wcons[sl] = kb.tok("pe")
    p.bproj_done = kb.snap() + p.bkv_ag
    for nm, d in (("d_qb", p.qb_d), ("d_kag", p.kb_ag), ("d_vag", p.vb_ag)):
        if nm in p.dbg_t:
            dump_dram(kb, p, p.dbg_t[nm], d, p.bproj_done)
    if "d_bound" in p.dbg_t:
        cpd = kb.op("vector", lambda e, r: e.tensor_copy(p.evs[:, 0:256], p.bnd[:, :]),
                    deps=p.bproj_done, inc="dve")
        dma(kb, "gpsimd", p.dbg_t["d_bound"].ap(), p.evs[:, 0:256], deps=[cpd], slot="dbg")
    if "d_kmax" in p.dbg_t:
        cpk = kb.op("vector", lambda e, r: e.tensor_copy(p.evs[0:1, 0:8], p.row[0:1, 8:16]),
                    deps=p.bproj_done, inc="dve")
        dma(kb, "gpsimd", p.dbg_t["d_kmax"].ap(), p.evs[0:1, 0:8], deps=[cpk], slot="dbg")


def attn_b_attend(kb, p):
    kb.fence()
    W2f = p.W2.ap().bitcast(F32)
    OACC = 4096
    z1 = kb.op("vector", lambda e, r: e.memset(W2f[:, OACC:OACC + 4096], 0.0),
               deps=p.bproj_done, inc="dve")
    p.boad = z1

    exp_busy = [None] * 3
    mst_busy = None
    sb_busy = [None, None]
    qcons = None
    lcons = None
    brcons = None
    for g in range(8):
        qldg = dma(kb, "sync", s3(p.W1[:, 0:4096], 32),
                   r3(p.qb_d.ap()[4 * g * 1024:(4 * g + 4) * 1024, :]),
                   deps=(p.bproj_done if g == 0 else []) + [qcons], slot="bq")
        for qsel in (1, 0):
            brld = []
            for h in range(4):
                brld.append(dma(kb, "sync", p.brst[h][0:1, :],
                                p.browd.ap()[(4 * g + h) * 2 + qsel:(4 * g + h) * 2 + qsel + 1, :],
                                deps=[brcons] + (p.bproj_done if g == 0 and qsel == 1 else []),
                                slot=f"bbr{h}"))
            nkc = 16 if qsel == 1 else 8
            for kc in range(nkc):
                sl = kc % 2
                r_, colb = chunk_owner(kc)
                dep0 = p.bproj_done if (g == 0 and qsel == 1 and kc < 2) else []
                kld = dma(kb, "sync", p.W2[:, 4096 + sl * 1024:4096 + sl * 1024 + 512],
                          p.kb_ag.ap()[1024 * r_ + 128 * g:1024 * r_ + 128 * (g + 1),
                                       colb:colb + 512], deps=[sb_busy[sl]] + dep0, slot=f"bak{sl}")
                vld = dma(kb, "sync", s3(p.W2[:, 4096 + sl * 1024 + 512:4096 + sl * 1024 + 1024], 4),
                          r3(p.vb_ag.ap()[1024 * r_ + colb:1024 * r_ + colb + 512,
                                          128 * g:128 * (g + 1)]), deps=[sb_busy[sl]] + dep0,
                          slot=f"bav{sl}")
                masked = (qsel == 0) or (kc >= 8)
                if masked:
                    mld = dma(kb, "sync", p.W2[:, sl * 2048:(sl + 1) * 2048],
                              p.bmasks.ap()[kc * 128:(kc + 1) * 128, :],
                              deps=[sb_busy[sl]] + dep0, slot=f"bam{sl}")
                for h in range(4):
                    psLb = p.psF[3] if h < 2 else p.psF[5]
                    lrow = (h % 2) * 32
                    exs = [None] * 4
                    msts = [None] * 4
                    for kt in range(4):
                        bank = kt % 3
                        kb.op("tensor", lambda e, r, bank=bank, h=h: e.matmul(
                            p.psF[bank][:, :], p.c_onesr[0:1, 0:128],
                            p.brst[h][0:1, :], start=True, stop=False),
                            deps=[exp_busy[bank], brld[h]], inc=None)
                        smm = kb.op("tensor", lambda e, r, bank=bank, h=h, kt=kt, sl=sl, qsel=qsel: e.matmul(
                            p.psF[bank][:, :],
                            p.W2[:, 4096 + sl * 1024 + kt * 128:4096 + sl * 1024 + (kt + 1) * 128],
                            p.W1[:, h * 1024 + qsel * 512:h * 1024 + (qsel + 1) * 512],
                            start=False, stop=True), deps=[kld, qldg], inc="pe")
                        ex = kb.op("scalar", lambda e, r, bank=bank, kt=kt: e.activation(
                            p.p_sb[:, kt * 512:(kt + 1) * 512], p.psF[bank][:, :], AF.Exp),
                            deps=[smm, mst_busy], inc="act")
                        exp_busy[bank] = ex
                        exs[kt] = ex
                        if masked:
                            mst = kb.op("vector", lambda e, r, kt=kt, sl=sl: e.tensor_tensor(
                                p.pm_sb[:, kt * 512:(kt + 1) * 512], p.p_sb[:, kt * 512:(kt + 1) * 512],
                                p.W2[:, sl * 2048 + kt * 512:sl * 2048 + (kt + 1) * 512], ALU.mult),
                                deps=[ex, mld, p.g("bmst2")], inc="dve")
                            p.bmst2 = mst
                            msts[kt] = mst
                    if masked:
                        mst_busy = p.bmst2
                    psrc = p.pm_sb if masked else p.p_sb
                    for kt in range(4):
                        kb.op("tensor", lambda e, r, kt=kt, kc=kc, nkc=nkc, psrc=psrc, psLb=psLb,
                              lrow=lrow: e.matmul(
                            psLb[lrow:lrow + 1, 0:512], p.c_onesc[:, :],
                            psrc[:, kt * 512:(kt + 1) * 512],
                            start=(kc == 0 and kt == 0), stop=(kc == nkc - 1 and kt == 3),
                            skip_group_check=True),
                            deps=[msts[kt] or exs[kt]] + ([lcons] if kc == 0 and kt == 0 else []),
                            inc=None)
                    pvl = None
                    for kt in range(4):
                        pvl = kb.op("tensor", lambda e, r, kt=kt, sl=sl, psrc=psrc: e.matmul(
                            p.psF[4][:, :],
                            p.W2[:, 4096 + sl * 1024 + 512 + kt * 128:4096 + sl * 1024 + 512 + (kt + 1) * 128],
                            psrc[:, kt * 512:(kt + 1) * 512], start=(kt == 0), stop=(kt == 3)),
                            deps=[msts[kt] or exs[kt], vld] + ([p.boad] if kt == 0 else []),
                            inc=("pe" if kt == 3 else None))
                    oad = kb.op("vector", lambda e, r, h=h, qsel=qsel: e.tensor_tensor(
                        W2f[:, OACC + h * 1024 + qsel * 512:OACC + h * 1024 + qsel * 512 + 512],
                        W2f[:, OACC + h * 1024 + qsel * 512:OACC + h * 1024 + qsel * 512 + 512],
                        p.psF[4][:, :], ALU.add), deps=[pvl, p.boad], inc="dve")
                    p.boad = oad
                sb_busy[sl] = kb.tok("pe")
            brcons = kb.tok("pe")
            getok = kb.tok("pe")
            ge_busy = None
            for h in range(4):
                psLb = p.psF[3] if h < 2 else p.psF[5]
                lrow = (h % 2) * 32
                rc = kb.op("vector", lambda e, r, psLb=psLb, lrow=lrow: e.reciprocal(
                    p.evs[0:1, 0:512], psLb[lrow:lrow + 1, 0:512]),
                    deps=[getok, ge_busy], inc="dve")
                bcm = kb.op("tensor", lambda e, r: e.matmul(
                    p.psF[4][:, :], p.c_onesrf[0:1, 0:128], p.evs[0:1, 0:512],
                    start=True, stop=True), deps=[rc, p.boad, p.g("bctxop")], inc="pe")
                ctx = kb.op("vector", lambda e, r, g=g, h=h, qsel=qsel: e.tensor_tensor(
                    p.A1[:, (4 * g + h) * 1024 + qsel * 512:(4 * g + h) * 1024 + (qsel + 1) * 512],
                    W2f[:, OACC + h * 1024 + qsel * 512:OACC + h * 1024 + (qsel + 1) * 512],
                    p.psF[4][:, :], ALU.mult), deps=[bcm], inc="dve")
                p.bctxop = ctx
                rs1 = kb.op("vector", lambda e, r, h=h, qsel=qsel: e.memset(
                    W2f[:, OACC + h * 1024 + qsel * 512:OACC + h * 1024 + (qsel + 1) * 512], 0.0),
                    deps=[ctx], inc="dve")
                ge_busy = rs1
                lcons = rc
            p.boad = ge_busy
        qcons = kb.tok("pe")
    p.batt_done = kb.snap()
    if "d_ctx" in p.dbg_t:
        bus = None
        for i in range(64):
            cpc = kb.op("vector", lambda e, r, i=i: e.tensor_copy(p.evs[:, 0:512],
                                                                  p.A1[:, i * 512:(i + 1) * 512]),
                        deps=p.batt_done + [bus], inc="dve")
            bus = dma(kb, "gpsimd", p.dbg_t["d_ctx"].ap()[:, i * 512:(i + 1) * 512], p.evs[:, 0:512],
                      deps=[cpc], slot="dbg")


def attn_b_oproj(kb, p):
    kb.fence()
    wcons = [None, None]
    ev_busy = [None, None]
    xst = {}
    for oc in range(8):
        half = oc % 2
        W = p.W1 if half == 0 else p.W2
        wld = dma(kb, "gpsimd", s3(W[:, 0:32 * 512], 32),
                  r3(p.wdo_d.ap()[:, oc * 512:(oc + 1) * 512]),
                  deps=p.batt_done + [wcons[half]], slot=f"ow{half}")
        for tt in range(8):
            bank = tt % 2
            mm = None
            for dm in range(32):
                mm = kb.op("tensor", lambda e, r, dm=dm, tt=tt, W=W, bank=bank: e.matmul(
                    p.psF[bank][:, :], p.A1[:, dm * 1024 + tt * 128:dm * 1024 + (tt + 1) * 128],
                    W[:, dm * 512:(dm + 1) * 512], start=(dm == 0), stop=(dm == 31)),
                    deps=([wld, ev_busy[bank]] if dm == 0 else []), inc=("pe" if dm == 31 else None))
            co = bank * 512
            xld = dma(kb, "sync", p.evs[:, co:co + 512],
                      p.xw.ap()[tt * 128:(tt + 1) * 128, oc * 512:(oc + 1) * 512],
                      deps=[ev_busy[bank]] + (p.batt_done if oc < 2 else []), slot=f"ox{bank}")
            ev = kb.op("vector", lambda e, r, bank=bank, co=co: e.tensor_tensor(
                p.evs2[:, co:co + 512], p.psF[bank][:, :], p.evs[:, co:co + 512], ALU.add),
                deps=[mm, xld, ev_busy[bank], xst.get(bank)], inc="dve")
            ev_busy[bank] = ev
            st = dma(kb, "sync", p.xw.ap()[tt * 128:(tt + 1) * 128, oc * 512:(oc + 1) * 512],
                     p.evs2[:, co:co + 512], deps=[ev], slot=f"oxs{bank}")
            xst[bank] = st
        wcons[half] = kb.tok("pe")
    p.sub3_done = kb.snap()




def dump_dram(kb, p, dst, src, deps):
    rows, cols = src.shape
    bus = None
    for r0 in range(0, rows, 128):
        for c0 in range(0, cols, 512):
            cc = min(512, cols - c0)
            ld = dma(kb, "gpsimd", p.evs[:, 0:cc], src.ap()[r0:r0 + 128, c0:c0 + cc],
                     deps=list(deps) + [bus], slot="dbg")
            bus = dma(kb, "gpsimd", dst.ap()[r0:r0 + 128, c0:c0 + cc], p.evs[:, 0:cc], deps=[ld],
                      slot="dbg")



_CACHE = {}


def prep_inputs(ins):
    st = stored_tokens()
    x_full = np.asarray(ins["L0_attn_a_hidden_state"])[0]
    wq = np.asarray(ins["L0_attn_a_wq"]); bq = np.asarray(ins["L0_attn_a_bq"])
    wk = np.asarray(ins["L0_attn_a_wk"]); bk = np.asarray(ins["L0_attn_a_bk"])
    wv = np.asarray(ins["L0_attn_a_wv"]); bv = np.asarray(ins["L0_attn_a_bv"])
    wo = np.asarray(ins["L0_attn_a_wo"]); bo = np.asarray(ins["L0_attn_a_bo"])
    A = np.asarray(ins["L0_ffn_A"]); B = np.asarray(ins["L0_ffn_B"])
    r4 = np.arange(4096)
    sig4 = 256 * (r4 // 256) + 2 * (r4 % 128) + (r4 % 256) // 128
    base = dict(
        wo_a=np.ascontiguousarray(wo[sig4]), bo_a=np.ascontiguousarray(bo[None, :]),
        wbT=np.ascontiguousarray(np.asarray(ins["L0_ffn_W_base"]).T),
        bb=np.ascontiguousarray(np.asarray(ins["L0_ffn_b_base"])[None, :]),
        wrT=np.ascontiguousarray(np.asarray(ins["L0_ffn_W_router"]).T),
        acT=np.ascontiguousarray(A.transpose(2, 0, 1).reshape(D, ER)),
        bcat=np.ascontiguousarray(B.transpose(0, 2, 1).reshape(ER, D)),
        qwq=np.ascontiguousarray(np.asarray(ins["L0_attn_b_qw_q"])),
        qwk=np.ascontiguousarray(np.asarray(ins["L0_attn_b_qw_k"])),
        qwv=np.ascontiguousarray(np.asarray(ins["L0_attn_b_qw_v"])),
        qwo=np.ascontiguousarray(np.asarray(ins["L0_attn_b_qw_o"])),
        scq=np.ascontiguousarray(np.asarray(ins["L0_attn_b_scales_q"])),
        sck=np.ascontiguousarray(np.asarray(ins["L0_attn_b_scales_k"])),
        scv=np.ascontiguousarray(np.asarray(ins["L0_attn_b_scales_v"])),
        sco=np.ascontiguousarray(np.asarray(ins["L0_attn_b_scales_o"])),
        ident=np.eye(128, dtype=ml_dtypes.bfloat16),
        identf=np.eye(128, dtype=np.float32),
        onesr=np.ones((1, 512), ml_dtypes.bfloat16),
        onesc=np.ones((128, 1), ml_dtypes.bfloat16),
        onesrf=np.ones((1, 128), np.float32),
        emat=np.ascontiguousarray(np.eye(8, dtype=np.float32).repeat(16, axis=1)).astype(ml_dtypes.bfloat16),
        rotm=host_rot().astype(ml_dtypes.bfloat16),
    )
    in_maps = []
    for c in range(NC):
        toks = st[core_stored_rows(c)]
        qcols = np.concatenate([np.arange(h * HD, (h + 1) * HD) for h in CORE_HEADS[c]])
        kcols = np.concatenate([np.arange(g * HD, (g + 1) * HD) for g in CORE_GROUPS[c]])
        cqc, cqs, ckc, cks = host_rope(toks)
        m = dict(base)
        m.update(x=np.ascontiguousarray(x_full[toks]),
                 wq_a=np.ascontiguousarray(wq[:, qcols]),
                 wk_a=np.ascontiguousarray(wk[:, kcols]),
                 wv_a=np.ascontiguousarray(wv[:, kcols]),
                 bq_a=np.ascontiguousarray(bq[qcols].reshape(4, 128).T),
                 bk_a=np.ascontiguousarray(bk[kcols].reshape(2, 128).T),
                 bv_a=np.ascontiguousarray(bv[kcols].reshape(2, 128).T),
                 ropeqc=cqc, ropeqs=cqs, ropekc=ckc, ropeks=cks,
                 bmasks=build_bmasks(c))
        in_maps.append(m)
    return in_maps, st


def kernel(**ins):
    key = (tuple(sorted(DBG)), PHASES, SUB1, ATT_H, ATT_QC, SKIP)
    if key not in _CACHE:
        _CACHE[key] = build_program(DBG)
    nc = _CACHE[key]
    in_maps, st = prep_inputs(ins)
    res = run_bass_kernel_spmd(nc, in_maps, core_ids=list(range(NC)))
    out = np.zeros((1, S, D), np.float32)
    for c in range(NC):
        out[0, st[core_stored_rows(c)], :] = np.asarray(res.results[c]["out"], np.float32)
    kernel.last = res.results
    return out
